# version 19
# baseline (speedup 1.0000x reference)
import numpy as np
from contextlib import ExitStack

import concourse.bass as bass
import concourse.mybir as mybir
from concourse.bass_utils import run_bass_kernel_spmd

F32 = mybir.dt.float32
F16 = mybir.dt.float16
F8 = mybir.dt.float8e4
BF16 = mybir.dt.bfloat16
I32 = mybir.dt.int32
AF = mybir.ActivationFunctionType
ALU = mybir.AluOpType
AX = mybir.AxisListType.X

NC = 8
B = 2
S = 8192
BS = B * S
D = 1024
NH = 16
H4 = 4096

TOK = 2048
HALO = 256
SLAB = TOK + 2 * HALO
NCH = SLAB // 128
OWN0 = 2
NOWN = 16
NW = 9
NST = 4

RING_HF = 6
RING_HT = 6

INV_D = 1.0 / D
EPS = 1e-5
ISQ = 0.125

ENGS = ("sync", "scalar", "vector", "tensor", "gpsimd")
CSEM = {"sync": "cy", "scalar": "cs", "vector": "cv", "tensor": "ct", "gpsimd": "cg"}
DSEM = {"sync": "dy", "scalar": "ds", "vector": "dv", "tensor": "dt", "gpsimd": "dg"}
SEMS = ("cy", "cs", "cv", "ct", "cg", "dy", "ds", "dv", "dt", "dg", "cc")

_TAPS = set()
_TAP_RESULTS = {}
_PROG_CACHE = {}
_STAGE = 6
_NO_RES_AG = False
_NO_STATS_AG = False
_NO_ACCUM = False


class Tok:
    __slots__ = ("sem", "val")

    def __init__(self, sem, val):
        self.sem = sem
        self.val = val


class Plan:
    def __init__(self):
        self.streams = {e: [] for e in ENGS}
        self.cnt = {s: 0 for s in SEMS}
        self.maxw = {e: {} for e in ENGS}
        self.last = {e: None for e in ENGS}

    def op(self, eng, emit, after=(), sem=None, n=1):
        if sem is None:
            sem = CSEM[eng]
        if eng in ("vector", "scalar"):
            after = tuple(after) + (self.last[eng],)
        waits = []
        for t in after:
            if t is None:
                continue
            if self.maxw[eng].get(t.sem, 0) >= t.val:
                continue
            self.maxw[eng][t.sem] = t.val
            waits.append((t.sem, t.val))
        self.cnt[sem] += n
        self.streams[eng].append((emit, tuple(waits), sem, n))
        tok = Tok(sem, self.cnt[sem])
        if eng in ("vector", "scalar"):
            self.last[eng] = tok
        return tok

    def dma(self, eng, out, in_, after=(), accum=False, slow=False):
        sem = DSEM[eng]
        kw = {}
        if accum:
            kw["accum_op"] = ALU.add
        if slow:
            kw["allow_slow_non_contiguous"] = True
        emit = lambda e, o=out, i=in_, k=dict(kw): e.dma_start(out=o, in_=i, **k)
        return self.op(eng, emit, after=after, sem=sem, n=16)

    def bar(self, eng, after):
        return self.op(eng, lambda e: e.nop(), after)

    def snap(self):
        return tuple(Tok(s, self.cnt[s]) for s in SEMS if self.cnt[s] > 0)

    def finalize(self):
        waited = set()
        for e, stream in self.streams.items():
            for emit, waits, sem, n in stream:
                for ws, wv in waits:
                    waited.add((ws, wv))
        new_streams = {}
        for e, stream in self.streams.items():
            out = []
            pre = {}
            for emit, waits, sem, n in stream:
                p = pre.get(sem, 0)
                if p and (sem, p) in waited and (sem, p) not in waits:
                    waits = tuple(waits) + ((sem, p),)
                pre[sem] = p + n
                out.append((emit, waits, sem, n))
            new_streams[e] = out
        self.streams = new_streams

    def replay(self, eng_name, eng, sems):
        for emit, waits, sem, n in self.streams[eng_name]:
            for ws, wv in waits:
                eng.wait_ge(sems[ws], wv)
            emit(eng).then_inc(sems[sem], n)


def _ap(t, offset, pattern):
    return bass.AP(tensor=t.tensor if isinstance(t, bass.AP) else t,
                   offset=offset, ap=pattern)


def build_program(taps=()):
    stage = _STAGE
    nc = bass.Bass("TRN2", target_bir_lowering=False)
    taps = set(taps)

    def I(name, shape, dt=F32):
        return nc.dram_tensor(name, shape, dt, kind="ExternalInput").ap()

    xslab = I("xslab", [SLAB, D])
    xds = I("xds", [128, BS])
    x0 = I("x0", [256, D])
    memT = I("memT", [128, 512])
    convwT = I("convwT", [4, D, D])
    convb = I("convb", [128, 8])
    mixw = I("mixw", [2 * D, D])
    mixb = I("mixb", [128, 8])
    outw = I("outw", [D, D])
    outb = I("outb", [D])
    w1T = I("w1T", [D, H4])
    b1c = I("b1c", [128, 32])
    w2T = I("w2T", [H4, D])
    b2 = I("b2", [D])
    masksc = I("masksc", [TOK, D])
    triT = I("triT", [128, 4 * NW])
    invden = I("invden", [128, 16])
    ident = I("ident32", [128, 128])
    identb = I("identb", [128, 128], BF16)
    bcidx = I("bcidx", [128, 32], I32)

    yout = nc.dram_tensor("y", [TOK, D], F32, kind="ExternalOutput").ap()

    stats_in = nc.dram_tensor("stats_in", [32, 128], F32).ap()
    stats_all = nc.dram_tensor("stats_all", [NC, 32, 128], F32).ap()
    res_in = [nc.dram_tensor("res_in%d" % b, [128, S], F32).ap() for b in range(B)]
    res_ab = nc.dram_tensor("res_ab", [B * NC, 128, S], F32).ap()
    mixw_hi = nc.dram_tensor("mixw_hi", [2 * D, D], BF16).ap()
    mixw_lo = nc.dram_tensor("mixw_lo", [2 * D, D], BF16).ap()
    outw_hi = nc.dram_tensor("outw_hi", [D, D], BF16).ap()
    outw_lo = nc.dram_tensor("outw_lo", [D, D], BF16).ap()
    x2d = nc.dram_tensor("x2d", [TOK, D], F32).ap()
    local_d = nc.dram_tensor("local_d", [TOK, D], F32).ap()
    glob_d = nc.dram_tensor("glob_d", [TOK, D], F32).ap()

    tap_d = {}
    for tn in taps:
        shp = {"h": [SLAB, D], "local": [TOK, D], "glob": [TOK, D],
               "bc": [8 * 128, TOK], "x2": [TOK, D], "gtT": [128, 1024],
               "hpre": [32 * 128, 512]}[tn]
        tap_d[tn] = nc.dram_tensor("tap_" + tn, shp, F32, kind="ExternalOutput").ap()
    tap_dma = []

    P = Plan()

    es = ExitStack()

    def sb(st, name, shape, dtype=F32):
        return st.enter_context(nc.sbuf_tensor(name, shape, dtype))

    idt = sb(es, "idt", [128, 128])
    idb = sb(es, "idb", [128, 128], BF16)
    stats = sb(es, "stats", [128, 64])
    trash = sb(es, "trash", [128, D])
    smm = sb(es, "smm", [128, 8])
    gtT_h = sb(es, "gtT_h", [128, 1024], BF16)
    gtT_l = sb(es, "gtT_l", [128, 1024], BF16)
    vg = sb(es, "vg", [128, 16 * 65])
    geL = [sb(es, "geL%d" % i, [128, 128]) for i in range(2)]
    triS = sb(es, "triS", [128, 4 * NW])
    invdS = sb(es, "invdS", [128, 16])
    convbS = sb(es, "convbS", [128, 8])
    mixbS = sb(es, "mixbS", [128, 8])
    outbbc = sb(es, "outbbc", [128, D])
    lnst = sb(es, "lnst", [128, 32])
    bcix = sb(es, "bcix", [128, 32], I32)
    xst = [sb(es, "xst%d" % i, [128, D]) for i in range(2)]

    ps = [es.enter_context(nc.psum_tensor("ps%d" % i, [128, 512], F32))
          for i in range(7)]
    psb = es.enter_context(nc.psum_tensor("psb", [128, 1024], BF16))

    def vcopy(dst, src, after=()):
        return P.op("vector", lambda e, d=dst, s=src: e.tensor_copy(d, s), after)

    def vmemset(dst, val, after=()):
        return P.op("vector", lambda e, d=dst, v=val: e.memset(d, v), after)

    def vts(dst, src, s1, s2, op0, op1=None, after=()):
        if op1 is None:
            return P.op("vector",
                        lambda e, d=dst, s=src, a=s1: e.tensor_scalar(d, s, a, None, op0),
                        after)
        return P.op("vector",
                    lambda e, d=dst, s=src, a=s1, b=s2: e.tensor_scalar(d, s, a, b, op0, op1),
                    after)

    def vtt(dst, a, b, op, after=()):
        return P.op("vector", lambda e, d=dst, x=a, y=b: e.tensor_tensor(d, x, y, op), after)

    def vstt(dst, in0, scal, in1, op0, op1, after=()):
        return P.op("vector",
                    lambda e, d=dst, x=in0, s=scal, y=in1: e.scalar_tensor_tensor(d, x, s, y, op0, op1),
                    after)

    def vrecip(dst, src, after=()):
        return P.op("vector", lambda e, d=dst, s=src: e.reciprocal(d, s), after)

    def vred(dst, src, after=()):
        return P.op("vector", lambda e, d=dst, s=src: e.reduce_sum(d, s, axis=AX), after)

    def act(dst, src, fn, bias=None, scale=None, accum=None, after=()):
        kw = {}
        if bias is not None:
            kw["bias"] = bias
        if scale is not None:
            kw["scale"] = scale
        if accum is not None:
            kw["accum_out"] = accum
        return P.op("scalar",
                    lambda e, d=dst, s=src, k=dict(kw): e.activation(d, s, fn, **k),
                    after)

    def mm(pt, lhsT, rhs, start, stop, after=()):
        return P.op("tensor",
                    lambda e, o=pt, l=lhsT, r=rhs, st=start, sp=stop:
                        e.matmul(o, l, r, start=st, stop=sp),
                    after)

    def tr(pt, src, after=(), bf=False, npart=128):
        iden = idb if bf else idt
        idap = iden[:] if npart == 128 else iden[0:npart, 0:npart]
        return P.op("tensor",
                    lambda e, o=pt, s=src, i=idap: e.transpose(o, s, i),
                    after)

    def rsqrt_chain(vcol, ucol, sdcol, rcol, acol, t_v):
        t1 = vts(ucol, vcol, EPS, None, ALU.add, after=(t_v,))
        t2 = act(sdcol, ucol, AF.Sqrt, after=(t1,))
        tk = vrecip(rcol, sdcol, (t2,))
        for _ in range(2):
            ta = vts(acol, rcol, rcol, None, ALU.mult, after=(tk,))
            tb = vtt(acol, acol, ucol, ALU.mult)
            tc = vts(acol, acol, -0.5, 1.5, ALU.mult, ALU.add)
            tk = vtt(rcol, rcol, acol, ALU.mult)
        return tk

    def ln_stats(xap, mcol, rcol, sc6, after=()):
        c_sum, c_sq = sc6[:, 0:1], sc6[:, 1:2]
        c_var, c_u, c_sd, c_a = sc6[:, 2:3], sc6[:, 3:4], sc6[:, 4:5], sc6[:, 5:6]
        t1 = vred(c_sum, xap, after)
        t2 = act(trash[:], xap, AF.Square, accum=c_sq, after=after)
        t3 = vts(mcol, c_sum, INV_D, None, ALU.mult)
        t4 = vts(c_var, c_sq, INV_D, None, ALU.mult, after=(t2,))
        t5 = vts(c_u, mcol, mcol, None, ALU.mult)
        t6 = vtt(c_var, c_var, c_u, ALU.subtract)
        return rsqrt_chain(c_var, c_u, c_sd, rcol, c_a, t6)

    def split(hi, lo, src, after=()):
        t1 = vcopy(hi, src, after)
        t2 = vtt(lo, src, hi, ALU.subtract)
        return t2

    t_id = P.dma("sync", idt[:], ident)
    t_idb = P.dma("sync", idb[:], identb)
    t_tri = P.dma("sync", triS[:], triT)
    t_invd = P.dma("sync", invdS[:], invden)
    t_cb = P.dma("sync", convbS[:], convb)
    t_mb = P.dma("sync", mixbS[:], mixb)
    t_ob = P.dma("sync", outbbc[:], _ap(outb, 0, [[0, 128], [1, D]]))
    t_bcix = P.dma("sync", bcix[:], bcidx)

    def epilogue():
        outs = []
        for j in range(16):
            outs.append(P.dma("scalar", yout[j * 128:(j + 1) * 128, :], trash[:],
                              after=(P.snap())))
        P.bar("sync", tuple(outs))
        return finish_replay(nc, P, es, sems_ctx)

    sems_ctx = []

    def finish_replay(nc, P, es, _sc):
        P.finalize()
        with es:
            with ExitStack() as ses:
                sems = {sn: ses.enter_context(nc.semaphore(sn)) for sn in SEMS}
                with nc.Block() as block:
                    @block.sync
                    def _(sync):
                        P.replay("sync", sync, sems)

                    @block.scalar
                    def _(scalar):
                        P.replay("scalar", scalar, sems)

                    @block.vector
                    def _(vector):
                        P.replay("vector", vector, sems)

                    @block.tensor
                    def _(tensor):
                        P.replay("tensor", tensor, sems)

                    @block.gpsimd
                    def _(gpsimd):
                        P.replay("gpsimd", gpsimd, sems)
        return nc

    if stage < 1:
        return epilogue()

    f0 = ExitStack()
    gtT = sb(f0, "gtT", [128, 1024])
    gtp = sb(f0, "gtp", [128, D])
    h0t = sb(f0, "h0t", [128, 2 * D])
    h0b = sb(f0, "h0b", [128, D], BF16)
    h0l = sb(f0, "h0l", [128, D], BF16)
    h0T_h = sb(f0, "h0T_h", [128, 2048], BF16)
    h0T_l = sb(f0, "h0T_l", [128, 2048], BF16)
    cwb = [sb(f0, "cwb%d" % i, [128, D]) for i in range(2)]
    cwh = [sb(f0, "cwh%d" % i, [128, D], BF16) for i in range(2)]
    cwl = [sb(f0, "cwl%d" % i, [128, D], BF16) for i in range(2)]

    t_memT = P.dma("sync",
                   _ap(gtT[:], 64, [[1024, 128], [128, 8], [1, 64]]),
                   memT)



    psb_tok = [None]
    t_h0_parts = []
    for i in range(2):
        tl = P.dma("sync", xst[i][:], x0[i * 128:(i + 1) * 128, :])
        tst = ln_stats(xst[i][:], stats[:, 40 + i:41 + i], stats[:, 42 + i:43 + i],
                       stats[:, 44:50], after=(tl,))
        ta = vts(h0t[:, i * D:(i + 1) * D], xst[i][:],
                 stats[:, 40 + i:41 + i], stats[:, 42 + i:43 + i],
                 ALU.subtract, ALU.mult)
        tsp = split(h0b[:], h0l[:], h0t[:, i * D:(i + 1) * D], (ta,))
        for half, srcb in ((0, h0b), (1, h0l)):
            tts = []
            for dc in range(8):
                tts.append(tr(psb[:, dc * 128:(dc + 1) * 128],
                              srcb[:, dc * 128:(dc + 1) * 128],
                              after=(tsp, t_idb, psb_tok[0]), bf=True))
            dstten = h0T_h if half == 0 else h0T_l
            dst = _ap(dstten[:], i * 128, [[2048, 128], [256, 8], [1, 128]])
            ev = vcopy(dst, psb[:], tuple(tts))
            psb_tok[0] = ev
            t_h0_parts.append(ev)
    t_h0 = P.bar("vector", tuple(t_h0_parts))

    def h0T_strided(hi, dc_in, k):
        ten = h0T_h if hi else h0T_l
        return _ap(ten[:], dc_in * 256 + k, [[2048, 128], [4, 64]])

    cw_tok = [None, None]
    conv_last = None
    n_conv = 4 * 8 * 3 * 8
    ic = 0
    for k in range(4):
        for dc_in in range(8):
            bi = (k * 8 + dc_in) % 2
            tl = P.dma("sync", cwb[bi][:], convwT[k, dc_in * 128:(dc_in + 1) * 128, :],
                       after=(cw_tok[bi],))
            th = vcopy(cwh[bi][:], cwb[bi][:], (tl,))
            tlo = vtt(cwl[bi][:], cwb[bi][:], cwh[bi][:], ALU.subtract)
            for (lh, rhi, dep) in ((cwh[bi], True, th), (cwh[bi], False, tlo),
                                   (cwl[bi], True, tlo)):
                rhs = h0T_strided(rhi, dc_in, k)
                for dg in range(8):
                    conv_last = mm(ps[5][:, dg * 64:(dg + 1) * 64],
                                   lh[:, dg * 128:(dg + 1) * 128], rhs,
                                   start=(ic == 0), stop=(ic == n_conv - 1),
                                   after=(dep, t_h0))
                    ic += 1
            cw_tok[bi] = P.bar("tensor", (conv_last,))
    t_conv = P.bar("tensor", (conv_last,))
    t_gt_parts = [act(gtT[:, dg * 128:dg * 128 + 64], ps[5][:, dg * 64:(dg + 1) * 64],
                      AF.Identity, bias=convbS[:, dg:dg + 1],
                      after=(t_conv, t_cb, t_memT)) for dg in range(8)]
    t_gt = P.bar("scalar", tuple(t_gt_parts))
    t_gts = split(gtT_h[:], gtT_l[:], gtT[:], (t_gt,))
    t_vg_ones = vmemset(_ap(vg[:], 64, [[16 * 65, 128], [65, 16]]), 1.0)
    t_gtp_parts = []
    for dg in range(8):
        pcol = (dg % 4) * 128
        tt = tr(ps[6][:, pcol:pcol + 128], gtT[:, dg * 128:(dg + 1) * 128],
                after=(t_gt, t_id, t_gtp_parts[-1] if t_gtp_parts else None))
        t_gtp_parts.append(vcopy(gtp[:, dg * 128:(dg + 1) * 128],
                                 ps[6][:, pcol:pcol + 128], (tt,)))
    t_gtp = P.bar("vector", tuple(t_gtp_parts))
    t_vg_parts = [vcopy(vg[:, h * 65:h * 65 + 64], gtp[:, h * 64:(h + 1) * 64],
                        (t_gtp, t_vg_ones)) for h in range(NH)]
    t_vg = P.bar("vector", tuple(t_vg_parts))
    t_ps6_0 = P.bar("tensor", (t_gtp,))
    if "gtT" in taps:
        tap_dma.append(P.dma("gpsimd", tap_d["gtT"], gtT[:], after=(t_gts, t_vg)))

    t_stats = [None] * NCH
    xst_tok = [None, None]

    def emit_ln1_stats(i):
        bi = i % 2
        tl = P.dma("sync", xst[bi][:], xslab[i * 128:(i + 1) * 128, :],
                   after=(xst_tok[bi],))
        t_stats[i] = ln_stats(xst[bi][:], stats[:, i:i + 1], stats[:, 20 + i:21 + i],
                              stats[:, 50:56], after=(tl,))
        xst_tok[bi] = t_stats[i]

    t_ag_stats = [None]

    def plan_stats_ag():
        after = tuple(t_stats[OWN0:OWN0 + NOWN])
        t1 = P.dma("gpsimd", _ap(stats_in, 0, [[1, 128], [128, 16]]),
                   stats[:, OWN0:OWN0 + 16], after=after, slow=True)
        t2 = P.dma("gpsimd", _ap(stats_in, 2048, [[1, 128], [128, 16]]),
                   stats[:, 20 + OWN0:20 + OWN0 + 16], after=after, slow=True)

        if _NO_STATS_AG:
            t_ag_stats[0] = P.bar("gpsimd", (t1, t2))
            return

        def emit_cc(e):
            return e.collective_compute(
                "AllGather", ALU.bypass, replica_groups=[list(range(NC))],
                ins=[stats_in.opt()], outs=[stats_all.opt()])
        t_ag_stats[0] = P.op("gpsimd", emit_cc, (t1, t2), sem="cc")

    if stage >= 2:
        for i in range(NCH):
            emit_ln1_stats(i)
        plan_stats_ag()

    wsp_tok = [cw_tok[0], cw_tok[1]]
    for (wsrc, whi, wlo_, nchunks) in ((mixw, mixw_hi, mixw_lo, 16),
                                       (outw, outw_hi, outw_lo, 8)):
        for kc in range(nchunks):
            bi = kc % 2
            tl = P.dma("sync", cwb[bi][:], wsrc[kc * 128:(kc + 1) * 128, :],
                       after=(wsp_tok[bi],))
            th = vcopy(cwh[bi][:], cwb[bi][:], (tl,))
            tlo = vtt(cwl[bi][:], cwb[bi][:], cwh[bi][:], ALU.subtract)
            t1 = P.dma("gpsimd", whi[kc * 128:(kc + 1) * 128, :], cwh[bi][:],
                       after=(th,))
            t2 = P.dma("gpsimd", wlo_[kc * 128:(kc + 1) * 128, :], cwl[bi][:],
                       after=(tlo,))
            wsp_tok[bi] = P.bar("vector", (t1, t2))
    t_wsplit = P.bar("vector", tuple(wsp_tok))

    t_front = P.snap()
    f0.close()
    if stage < 2:
        return epilogue()

    pa = ExitStack()
    hf = [sb(pa, "hf%d" % i, [128, 16 * 65]) for i in range(RING_HF)]
    hsb = sb(pa, "hsb", [128, D], BF16)
    hsl = sb(pa, "hsl", [128, D], BF16)
    hTh = sb(pa, "hTh", [128, RING_HT * D], BF16)
    hTl = sb(pa, "hTl", [128, RING_HT * D], BF16)
    expl = [sb(pa, "expl%d" % i, [128, 512]) for i in range(8)]
    avt = [sb(pa, "avt%d" % i, [128, 512]) for i in range(2)]
    fbuf = [sb(pa, "fbuf%d" % i, [128, 260]) for i in range(4)]
    lacc = [sb(pa, "lacc%d" % i, [128, D]) for i in range(4)]
    gacc = [sb(pa, "gacc%d" % i, [128, D]) for i in range(2)]
    bcur = sb(pa, "bcur", [128, S])
    bnxt = sb(pa, "bnxt", [128, S])
    bmr = sb(pa, "bmr", [128, 2048])

    tF = P.bar("sync", t_front)
    t_hfones = [vmemset(_ap(hf[i][:], 64, [[16 * 65, 128], [65, 16]]), 1.0,
                        (tF,)) for i in range(RING_HF)]

    def hf65(slot):
        return _ap(hf[slot][:], 0, [[16 * 65, 128], [65, 16], [1, 64]])

    t_happly = [None] * NCH
    t_hT = [None] * NCH
    hf_war = [t_hfones[i] for i in range(RING_HF)]
    hT_war = [None] * RING_HT
    xst_tok2 = [tF, tF]

    def emit_ln1_chunk(i):
        bi = i % 2
        tl = P.dma("sync", xst[bi][:], xslab[i * 128:(i + 1) * 128, :],
                   after=(xst_tok2[bi],))
        slot = i % RING_HF
        ta = vts(hf65(slot), xst[bi][:], stats[:, i:i + 1], stats[:, 20 + i:21 + i],
                 ALU.subtract, ALU.mult, after=(hf_war[slot], tl))
        t_happly[i] = ta
        xst_tok2[bi] = ta
        if "h" in taps:
            tt_ = P.dma("gpsimd", tap_d["h"][i * 128:(i + 1) * 128, :],
                        hf65(slot), after=(ta,))
            tap_dma.append(tt_)
            hf_war[slot] = P.bar("gpsimd", (tt_, hf_war[slot]))
        tslot = i % RING_HT
        tsp = split(hsb[:], hsl[:], hf65(slot), (ta,))
        t_parts = []
        for half, srcb in ((0, hsb), (1, hsl)):
            tts = [tr(psb[:, dc * 128:(dc + 1) * 128],
                      srcb[:, dc * 128:(dc + 1) * 128],
                      after=(tsp, t_idb, hT_war[tslot], psb_tok[0]), bf=True)
                   for dc in range(8)]
            dst = hTh if half == 0 else hTl
            ev = vcopy(dst[:, tslot * D:(tslot + 1) * D], psb[:], tuple(tts))
            psb_tok[0] = ev
            t_parts.append(ev)
        t_hT[i] = P.bar("vector", tuple(t_parts))
        hT_war[tslot] = Tok("ct", P.cnt["ct"])

    SH = [1, 2, 4, 8, 16, 32, 64, 128, 256, 512, 1024, 2048, 4096]
    t_res_ag = [None]

    class Bcast:
        def __init__(self):
            self.state = "apply"
            self.bb = 0
            self.j = 0
            self.it = 0
            self.tcur = None
            self.tmr = None
            self.tstore = [None, None]
            self.cur = bcur
            self.nxt = bnxt
            self.res_done = [None, None]
            self.pieces = []
            self.iter_toks = []
            self.done = False

        def quantum(self):
            if self.done:
                return False
            bb = self.bb
            if self.state == "apply":
                j = self.j
                rk = bb * 4 + j
                sl = slice(j * 2048, (j + 1) * 2048)
                t_x = P.dma("sync", self.cur[:, sl],
                            xds[:, bb * S + j * 2048: bb * S + (j + 1) * 2048],
                            after=(self.tmr, self.tstore[0], self.tstore[1]))
                t_m = P.dma("gpsimd", bmr[:],
                            _ap(stats_all, rk * 4096, [[0, 128], [1, 2048]]),
                            after=(t_ag_stats[0], self.tmr))
                t1 = vtt(self.cur[:, sl], self.cur[:, sl], bmr[:], ALU.subtract,
                         (t_m, t_x))
                t_r = P.dma("gpsimd", bmr[:],
                            _ap(stats_all, rk * 4096 + 2048, [[0, 128], [1, 2048]]),
                            after=(t1,))
                t2 = vtt(self.cur[:, sl], self.cur[:, sl], bmr[:], ALU.mult,
                         (t_r,))
                self.tmr = t2
                self.j += 1
                if self.j == 4:
                    self.j = 0
                    self.it = 0
                    self.tcur = t2
                    self.state = "iter"
                return True
            if not self.pieces:
                sh = SH[self.it]
                war = self.tstore[0]
                self.iter_toks = []
                CH = 2048
                if sh == 4096:
                    for a0 in (0, 4096):
                        s0 = (a0 + 4096) % S
                        for off in range(0, 4096, CH):
                            self.pieces.append((a0 + off, a0 + off, s0 + off,
                                                CH, war, False))
                else:
                    for (a0, ln0) in ((0, sh), (sh, S - 2 * sh), (S - sh, sh)):
                        m0 = (a0 - sh) % S
                        p0 = (a0 + sh) % S
                        for off in range(0, ln0, CH):
                            self.pieces.append((a0 + off, m0 + off, p0 + off,
                                                min(CH, ln0 - off), war, True))
            budget = 4096
            while self.pieces and budget > 0:
                a, am, ap_, ln, war, has_stt = self.pieces.pop(0)
                t1 = vtt(self.nxt[:, a:a + ln], self.cur[:, am:am + ln],
                         self.cur[:, ap_:ap_ + ln], ALU.add,
                         (self.tcur, war))
                if has_stt:
                    t1 = vstt(self.nxt[:, a:a + ln], self.nxt[:, a:a + ln],
                              0.5, self.cur[:, a:a + ln], ALU.mult, ALU.add,
                              (t1,))
                budget -= ln * (2 if has_stt else 1)
                self.iter_toks.append(t1)
            if self.pieces:
                return True
            tall = P.bar("vector", tuple(self.iter_toks))
            acc = (self.it > 0 and not _NO_ACCUM)
            sts = [P.dma("gpsimd",
                         res_in[bb][:, q * 2048:(q + 1) * 2048],
                         self.nxt[:, q * 2048:(q + 1) * 2048],
                         after=(tall, self.tstore[1]), accum=acc)
                   for q in range(4)]
            ts_ = P.bar("gpsimd", tuple(sts))
            self.tstore = [ts_, self.tstore[0]]
            self.tcur = tall
            self.cur, self.nxt = self.nxt, self.cur
            self.it += 1
            if self.it == len(SH):
                if _NO_RES_AG:
                    self.res_done[bb] = P.bar("gpsimd", (ts_,))
                else:
                    def emit_cc(e, b=bb):
                        return e.collective_compute(
                            "AllGather", ALU.bypass,
                            replica_groups=[list(range(NC))],
                            ins=[res_in[b].opt()],
                            outs=[_ap(res_ab, b * NC * 128 * S,
                                      [[1, NC * 128 * S]])])
                    self.res_done[bb] = P.op("gpsimd", emit_cc, (ts_,), sem="cc")
                self.bb += 1
                self.state = "apply"
                self.tmr = tall
                if self.bb == B:
                    self.done = True
            return True

        def finish(self):
            while self.quantum():
                pass
            t_res_ag[0] = P.bar("gpsimd", tuple(self.res_done))

    bc = Bcast()

    t_lacc = {}
    lacc_war = [None] * 4
    gacc_war = [None] * 2
    t_lspill = [None] * NOWN
    t_gspill = [None] * NOWN
    expl_war = [None] * 4
    av_war = [None, None]
    ps45_tok = [None, None]
    gexp_war = [None, None]
    gav_war = [None, None]
    gn_war = [None, None]
    lacc_init = {}
    ps4_tok = [None]
    ps6_tok = [t_ps6_0]
    fbuf_war = [None] * 4

    def emit_window(w, filler=None):
        c0 = 2 * w
        qc0 = max(c0, OWN0)
        qc1 = min(c0 + 4, OWN0 + NOWN)
        nq = (qc1 - qc0) * 128

        def logits_exp(h):
            dc = h // 2
            prow = 64 * (h % 2)
            par = h % 2
            for kc in range(4):
                c = c0 + kc
                pt = ps[kc]
                kslot = c % RING_HT
                runs = []
                cc_ = qc0
                while cc_ < qc1:
                    r0_ = cc_
                    while (cc_ + 1 < qc1 and
                           (cc_ + 1) % RING_HT == (cc_ % RING_HT) + 1):
                        cc_ += 1
                    runs.append((r0_, cc_ + 1))
                    cc_ += 1
                last = None
                np_i = 0
                n_tot = len(runs) * 3
                for (ra, rb) in runs:
                    nrun = rb - ra
                    rslot = ra % RING_HT
                    for (lhi, rhi) in ((True, True), (True, False), (False, True)):
                        lten = hTh if lhi else hTl
                        rten = hTh if rhi else hTl
                        lhsT = lten[prow:prow + 64,
                                    kslot * D + dc * 128: kslot * D + dc * 128 + 128]
                        sl0 = rten[prow:prow + 64, 0:1]
                        rhs = bass.AP(tensor=sl0.tensor,
                                      offset=sl0.offset + rslot * D + dc * 128,
                                      ap=[list(sl0.ap[0]), [D, nrun], [1, 128]])
                        qcol = (ra - qc0) * 128
                        last = mm(pt[:, qcol:qcol + nrun * 128], lhsT, rhs,
                                  start=(np_i == 0), stop=(np_i == n_tot - 1),
                                  after=(t_hT[c], t_hT[rb - 1], expl_war[kc]))
                        np_i += 1
                expl_war[kc] = act(expl[4 * par + kc][:, 0:nq], pt[:, 0:nq],
                                   AF.Exp, scale=ISQ,
                                   after=(last, av_war[par]))

        def av(h, nq_h):
            par = h % 2
            av_last = None
            for kc in range(4):
                c = c0 + kc
                av_last = mm(ps[4 + par][0:65, 0:nq_h],
                             hf[c % RING_HF][:, h * 65:h * 65 + 65],
                             expl[4 * par + kc][:, 0:nq_h],
                             start=(kc == 0), stop=(kc == 3),
                             after=(expl_war[kc], t_happly[c],
                                    t_hfones[c % RING_HF], ps45_tok[par]))
            av_war[par] = av_last
            tev = act(avt[par][0:65, 0:nq_h], ps[4 + par][0:65, 0:nq_h],
                      AF.Identity, after=(av_last,))
            ps45_tok[par] = tev
            return tev

        def fold(h, tev):
            par = h % 2
            k4 = h % 4
            nj = qc1 - qc0
            tts = [tr(ps[6][:, 65 * j:65 * j + 65],
                      avt[par][0:65, j * 128:(j + 1) * 128],
                      after=(tev, t_id, ps6_tok[0]), npart=65)
                   for j in range(nj)]
            tevf = act(fbuf[k4][:, 0:65 * nj], ps[6][:, 0:65 * nj],
                       AF.Identity, after=tuple(tts) + (fbuf_war[k4],))
            ps6_tok[0] = tevf
            trc = vrecip(smm[:, 0:nj],
                         _ap(fbuf[k4][:], 64, [[260, 128], [65, nj]]),
                         (tevf,))
            jj0 = qc0 - c0
            tsc = vtt(smm[:, 4:4 + nj], smm[:, 0:nj],
                      triS[:, 4 * w + jj0:4 * w + jj0 + nj], ALU.mult,
                      (t_tri,))
            last = tsc
            for j in range(nj):
                cq = qc0 + j
                o = cq - OWN0
                slot = o % 4
                init_tok = lacc_init.get(cq)
                if init_tok is None:
                    init_tok = vmemset(lacc[slot][:], 0.0, (lacc_war[slot],))
                    lacc_init[cq] = init_tok
                last = vstt(lacc[slot][:, h * 64:(h + 1) * 64],
                            fbuf[k4][:, 65 * j:65 * j + 64],
                            smm[:, 4 + j:5 + j],
                            lacc[slot][:, h * 64:(h + 1) * 64],
                            ALU.mult, ALU.add,
                            after=(init_tok, t_lacc.get((cq, h))))
                t_lacc[(cq, h)] = last
            fbuf_war[k4] = last

        pend = {}
        for h in range(NH + 2):
            if h < NH:
                logits_exp(h)
            if 1 <= h <= NH:
                pend[h - 1] = av(h - 1, nq)
            if h >= 2:
                fold(h - 2, pend.pop(h - 2))
            if h >= 2 and filler is not None:
                filler()

    def spill_local(c):
        o = c - OWN0
        slot = o % 4
        after = tuple(t_lacc.get((c, h)) for h in range(NH))
        tl = vts(lacc[slot][:], lacc[slot][:], invdS[:, o:o + 1], None,
                 ALU.mult, after=after + (t_invd,))
        dst = tap_d["local"] if "local" in taps else local_d
        td = P.dma("gpsimd", dst[o * 128:(o + 1) * 128, :], lacc[slot][:],
                   after=(tl,))
        t_lspill[o] = td
        lacc_war[slot] = td

    def emit_glob(c):
        o = c - OWN0
        slot = o % 2
        t_init = vmemset(gacc[slot][:], 0.0, (gacc_war[slot],))
        kslot = c % RING_HT

        def glogits(h):
            par = h % 2
            dc = h // 2
            prow = 64 * (h % 2)
            last = None
            for pi, (lhi, rhi) in enumerate(((True, True), (True, False),
                                             (False, True))):
                lten = gtT_h if lhi else gtT_l
                rten = hTh if rhi else hTl
                last = mm(ps[par][:, 0:128],
                          lten[prow:prow + 64, dc * 128:(dc + 1) * 128],
                          rten[prow:prow + 64,
                               kslot * D + dc * 128: kslot * D + dc * 128 + 128],
                          start=(pi == 0), stop=(pi == 2),
                          after=(t_gts, t_hT[c], gexp_war[par]))
            gexp_war[par] = act(geL[par][:], ps[par][:, 0:128], AF.Exp,
                                scale=ISQ, after=(last, gav_war[par]))

        def gav(h):
            par = h % 2
            tav = mm(ps[2 + par][:, 0:65], geL[par][:],
                     vg[:, h * 65:h * 65 + 65],
                     start=True, stop=True,
                     after=(gexp_war[par], t_vg, gn_war[par]))
            gav_war[par] = tav
            trc = vrecip(smm[:, 2:3], ps[2 + par][:, 64:65], (tav,))
            tn = vts(gacc[slot][:, h * 64:(h + 1) * 64], ps[2 + par][:, 0:64],
                     smm[:, 2:3], None, ALU.mult, after=(t_init,))
            gn_war[par] = tn
            return tn

        tn = None
        for h in range(NH + 1):
            if h < NH:
                glogits(h)
            if h >= 1:
                tn = gav(h - 1)
        expl_war[0] = gexp_war[0]
        expl_war[1] = gexp_war[1]
        expl_war[2] = gn_war[0]
        expl_war[3] = gn_war[1]
        dst = tap_d["glob"] if "glob" in taps else glob_d
        td = P.dma("gpsimd", dst[o * 128:(o + 1) * 128, :], gacc[slot][:],
                   after=(tn,))
        t_gspill[o] = td
        gacc_war[slot] = td

    if stage >= 3:
        for i in range(NCH + 2):
            if i < NCH:
                emit_ln1_chunk(i)
            if i >= 5 and i % 2 == 1:
                w = (i - 5) // 2
                filler = (lambda: bc.quantum()) if stage >= 4 else None
                emit_window(w, filler)
                for c in (2 * w, 2 * w + 1):
                    if OWN0 <= c < OWN0 + NOWN:
                        spill_local(c)
                        emit_glob(c)
    if stage >= 4:
        bc.finish()

    if stage < 5:
        pa.close()
        return epilogue()
    t_phaseA = P.snap()
    pa.close()

    pw = ExitStack()
    w1q = sb(pw, "w1q", [128, 8 * H4], F8)
    t_w1q = [P.dma("gpsimd", _ap(w1q[:], 0, [[8 * H4, 128], [H4, 8], [1, H4]]),
                   _ap(w1T, 0, [[H4, 128], [128 * H4, 8], [1, H4]]),
                   after=t_phaseA)]

    pb = ExitStack()
    lst = sb(pb, "lst", [128, 512])
    gst = sb(pb, "gst", [128, 512])
    mixwR_h = sb(pb, "mixwR_h", [128, 16 * D], BF16)
    outwR_h = sb(pb, "outwR_h", [128, 8 * D], BF16)
    wlo = [sb(pb, "wlo%d" % i, [128, D], BF16) for i in range(2)]
    catTh = sb(pb, "catTh", [128, 16 * 512], BF16)
    catTl = sb(pb, "catTl", [128, 16 * 512], BF16)
    gateT = sb(pb, "gateT", [128, 512])
    lT = sb(pb, "lT", [128, 512])
    gT = sb(pb, "gT", [128, 512])
    bcT = sb(pb, "bcT", [128, 512])
    mixT = sb(pb, "mixT", [128, 512])
    mixTh = sb(pb, "mixTh", [128, 8 * 512], BF16)
    mixTl = sb(pb, "mixTl", [128, 8 * 512], BF16)
    aof = sb(pb, "aof", [128, D])
    x2c = sb(pb, "x2c", [128, D])
    xrs = sb(pb, "xrs", [128, D])

    local_src = tap_d["local"] if "local" in taps else local_d
    glob_src = tap_d["glob"] if "glob" in taps else glob_d

    tA = P.bar("sync", t_phaseA)
    t_mwr = P.dma("sync", mixwR_h[:],
                  _ap(mixw_hi, 0, [[D, 128], [128 * D, 16], [1, D]]), after=(tA,))
    t_owr = P.dma("sync", outwR_h[:],
                  _ap(outw_hi, 0, [[D, 128], [128 * D, 8], [1, D]]), after=(tA,))
    t_x2st = [None] * NST
    mixed_war = [None]
    gate_war = [None]
    x2c_war = [tA]
    xrs_war = [tA]
    lg_tok = [tA, tA]
    wlo_tok = [tA, tA]
    ps67_tok = [ps6_tok[0]]
    cat_war = [None]

    def load_lg(buf, src_d, o0, dcc, after):
        src = _ap(src_d, o0 * 128 * D + dcc * 128,
                  [[D, 128], [128 * D, 4], [1, 128]])
        return P.dma("sync", buf[:], src, after=after)

    def emit_supertile(st_):
        ch0 = OWN0 + 4 * st_
        o0 = 4 * st_
        spills = tuple(t_lspill[o0:o0 + 4] + t_gspill[o0:o0 + 4])
        t_cat = []
        lst2 = [lst, gst]
        for kc in range(16):
            src_d = local_src if kc < 8 else glob_src
            dcc = kc % 8
            par = kc % 2
            pt = ps[4 + par]
            tld = load_lg(lst2[par], src_d, o0, dcc, spills + (lg_tok[par],))
            tt_list = [tr(pt[:, j * 128:(j + 1) * 128],
                          lst2[par][:, j * 128:(j + 1) * 128],
                          after=(tld, t_id, ps45_tok[par]))
                       for j in range(4)]
            sl = slice(kc * 512, (kc + 1) * 512)
            th = vcopy(catTh[:, sl], pt[:, 0:512], tuple(tt_list) + (cat_war[0],))
            tl2 = vtt(catTl[:, sl], pt[:, 0:512], catTh[:, sl], ALU.subtract)
            ps45_tok[par] = tl2
            lg_tok[par] = P.bar("tensor", (tl2,))
            t_cat.append(tl2)
        t_catall = P.bar("vector", tuple(t_cat))
        t_split_done = []
        for grp in range(2):
            gate_last = None
            mm_all = []
            for kc in range(16):
                bi = kc % 2
                tlo = P.dma("sync", wlo[bi][:],
                            mixw_lo[kc * 128:(kc + 1) * 128, :],
                            after=(wlo_tok[bi],))
                sl = slice(kc * 512, (kc + 1) * 512)
                mmts = []
                for dg in range(4):
                    dout = grp * 4 + dg
                    wcol = kc * D + dout * 128
                    for p_, (lt_, rt_) in enumerate((
                            (mixwR_h[:, wcol:wcol + 128], catTh[:, sl]),
                            (mixwR_h[:, wcol:wcol + 128], catTl[:, sl]),
                            (wlo[bi][:, dout * 128:(dout + 1) * 128],
                             catTh[:, sl]))):
                        gate_last = mm(
                            ps[dg][:, 0:512], lt_, rt_,
                            start=(kc == 0 and p_ == 0),
                            stop=(kc == 15 and p_ == 2),
                            after=(t_mwr, tlo, t_catall) + tuple(t_split_done))
                        mmts.append(gate_last)
                mm_all.extend(mmts)
                wlo_tok[bi] = P.bar("tensor", (gate_last,))
            t_gmm = P.bar("tensor", (gate_last,))
            for dg in range(4):
                dout = grp * 4 + dg
                tg = act(gateT[:], ps[dg][:, 0:512], AF.Sigmoid,
                         bias=mixbS[:, dout:dout + 1],
                         after=(t_gmm, t_mb, gate_war[0]))
                lsl = slice(dout * 512, (dout + 1) * 512)
                gsl = slice((8 + dout) * 512, (9 + dout) * 512)
                te1 = vtt(lT[:], catTh[:, lsl], catTl[:, lsl], ALU.add, (t_catall,))
                te2 = vtt(gT[:], catTh[:, gsl], catTl[:, gsl], ALU.add)
                ixcol = st_ * 8 + dout
                tbc = P.op(
                    "gpsimd",
                    lambda e, col=ixcol: e.indirect_dma_start(
                        out=bcT[:], out_offset=None,
                        in_=_ap(res_ab, 0, [[512, B * NC * 128 * (S // 512)], [1, 512]]),
                        in_offset=bass.IndirectOffsetOnAxis(
                            ap=bcix[:, col:col + 1], axis=0)),
                    after=(t_res_ag[0], t_bcix, mixed_war[0]), sem="dg", n=16)
                tm1 = vtt(mixT[:], lT[:], gT[:], ALU.subtract)
                tm2 = vtt(mixT[:], gateT[:], mixT[:], ALU.mult, (tg,))
                gate_war[0] = tm2
                tm3 = vtt(mixT[:], mixT[:], gT[:], ALU.add)
                tm4 = vstt(mixT[:], bcT[:], 1.0 / 14.0, mixT[:], ALU.mult,
                           ALU.add, (tbc,))
                mixed_war[0] = tm4
                tsp2 = split(mixTh[:, dout * 512:(dout + 1) * 512],
                             mixTl[:, dout * 512:(dout + 1) * 512], mixT[:])
                t_split_done.append(tsp2)
                if "bc" in taps:
                    tap_dma.append(P.dma("gpsimd",
                                         tap_d["bc"][dout * 128:(dout + 1) * 128,
                                                     st_ * 512:(st_ + 1) * 512],
                                         bcT[:], after=(tm4,)))
        cat_war[0] = t_split_done[-1]
        t_mix = P.bar("vector", tuple(t_split_done))
        x2toks = []
        for j in range(4):
            cq = ch0 + j
            last = None
            for kc in range(8):
                bi = kc % 2
                tlo = P.dma("sync", wlo[bi][:],
                            outw_lo[kc * 128:(kc + 1) * 128, :],
                            after=(wlo_tok[bi],))
                for nh_ in range(2):
                    pt = ps[5 + nh_]
                    for p_, lt_ in enumerate((mixTh, mixTl, mixTh)):
                        lhsT = lt_[:, kc * 512 + j * 128: kc * 512 + (j + 1) * 128]
                        rhs = (wlo[bi][:, nh_ * 512:(nh_ + 1) * 512] if p_ == 2
                               else outwR_h[:, kc * D + nh_ * 512: kc * D + (nh_ + 1) * 512])
                        last = mm(pt[:, 0:512], lhsT, rhs,
                                  start=(kc == 0 and p_ == 0),
                                  stop=(kc == 7 and p_ == 2),
                                  after=(t_owr, tlo, t_mix, ps67_tok[0],
                                         ps45_tok[1]))
                wlo_tok[bi] = P.bar("tensor", (last,))
            t_ao = P.bar("tensor", (last,))
            ta1 = vtt(aof[:, 0:512], ps[5][:, 0:512], outbbc[:, 0:512], ALU.add,
                      (t_ao, t_ob))
            ta2 = vtt(aof[:, 512:1024], ps[6][:, 0:512], outbbc[:, 512:1024],
                      ALU.add)
            ps67_tok[0] = ta2
            tst = ln_stats(aof[:], lnst[:, 0:1], lnst[:, 1:2], lnst[:, 2:8],
                           after=(ta2,))
            trs = P.dma("sync", xrs[:], xslab[cq * 128:(cq + 1) * 128, :],
                        after=(xrs_war[0],))
            tt1 = vts(x2c[:], aof[:], lnst[:, 0:1], lnst[:, 1:2],
                      ALU.subtract, ALU.mult, after=(tst, x2c_war[0]))
            tt2 = vtt(x2c[:], x2c[:], xrs[:], ALU.add, (trs,))
            xrs_war[0] = tt2
            tso = P.dma("gpsimd", x2d[(cq - OWN0) * 128:(cq - OWN0 + 1) * 128, :],
                        x2c[:], after=(tt2,))
            x2c_war[0] = tso
            x2toks.append(tso)
            if "x2" in taps:
                tap_dma.append(P.dma("gpsimd",
                                     tap_d["x2"][(cq - OWN0) * 128:(cq - OWN0 + 1) * 128, :],
                                     x2c[:], after=(tso,)))
        t_x2st[st_] = P.bar("gpsimd", tuple(x2toks))

    for st_ in range(NST):
        emit_supertile(st_)

    t_phaseB = P.snap()
    pb.close()
    if stage < 6:
        return epilogue()

    pf = ExitStack()
    w2q = sb(pf, "w2q", [128, 32 * D], F8)
    b1S = sb(pf, "b1S", [128, 32])
    b1q8 = sb(pf, "b1q8", [128, 32], F8)
    b2bc = sb(pf, "b2bc", [128, D])
    b2q8 = sb(pf, "b2q8", [128, D], F8)
    x2s = [sb(pf, "x2s%d" % i, [128, D]) for i in range(8)]
    xf = sb(pf, "xf", [128, D])
    xt8 = [sb(pf, "xt8_%d" % i, [128, 8 * 512], F8) for i in range(2)]
    h16 = [sb(pf, "h16_%d" % i, [128, 512], F16) for i in range(2)]
    h8 = sb(pf, "h8", [128, 32 * 512], F8)
    y16 = sb(pf, "y16", [128, D], F16)
    ydf = sb(pf, "ydf", [128, D])
    msk = [sb(pf, "msk%d" % i, [128, D]) for i in range(2)]
    yo = sb(pf, "yo", [128, D])

    pA = t_phaseB
    t_b1l = P.dma("sync", b1S[:], b1c, after=pA)
    tb1a = vcopy(b1q8[:], b1S[:], (t_b1l,))
    t_b1 = vcopy(b1S[:], b1q8[:])
    t_b2l = P.dma("sync", b2bc[:], _ap(b2, 0, [[0, 128], [1, D]]), after=pA)
    tb2a = vcopy(b2q8[:], b2bc[:], (t_b2l,))
    t_b2 = vcopy(b2bc[:], b2q8[:])

    t_w2q = P.dma("gpsimd", _ap(w2q[:], 0, [[32 * D, 128], [D, 32], [1, D]]),
                  _ap(w2T, 0, [[D, 128], [128 * D, 32], [1, D]]),
                  after=pA)
    t_wq = P.bar("gpsimd", (t_w1q[0], t_w2q))

    t_out_dma = list(tap_dma)
    msk_tok = [None, None]
    x2s_tok = [None] * 8
    h16_tok = [Tok("ct", P.cnt["ct"]), Tok("ct", P.cnt["ct"])]
    xf_war = [None]
    y_war = [None]
    ps4_tok_f = [ps45_tok[0], ps45_tok[1]]
    h8_war = [None, None]
    t_xr = [None] * NST

    def build_x(st_):
        par = st_ % 2
        txt = []
        for j in range(4):
            cq = 4 * st_ + j
            jb = 4 * par + j
            tl = P.dma("sync", x2s[jb][:], x2d[cq * 128:(cq + 1) * 128, :],
                       after=(t_x2st[st_], x2s_tok[jb]))
            tst = ln_stats(x2s[jb][:], lnst[:, 8:9], lnst[:, 9:10], lnst[:, 10:16],
                           after=(tl,))
            ta = vts(xf[:], x2s[jb][:], lnst[:, 8:9], lnst[:, 9:10],
                     ALU.subtract, ALU.mult, after=(tst, xf_war[0]))
            parts = []
            for dg4 in range(2):
                pt = ps[4 + dg4]
                tts = [tr(pt[:, dc4 * 128:(dc4 + 1) * 128],
                          xf[:, (dg4 * 4 + dc4) * 128:(dg4 * 4 + dc4 + 1) * 128],
                          after=(ta, t_id, ps4_tok_f[dg4]))
                       for dc4 in range(4)]
                dst = _ap(xt8[par][:], (dg4 * 4) * 512 + j * 128,
                          [[8 * 512, 128], [512, 4], [1, 128]])
                tq = vcopy(dst, pt[:, 0:512], tuple(tts) + (h8_war[par],))
                ps4_tok_f[dg4] = tq
                parts.append(tq)
            txt.append(P.bar("vector", tuple(parts)))
            xf_war[0] = txt[-1]
        t_xr[st_] = P.bar("vector", tuple(txt))

    build_x(0)
    for st_ in range(NST):
        par = st_ % 2
        t_h8_parts = []
        for hc in range(32):
            bi = hc % 2
            last = None
            for kc in range(8):
                last = mm(ps[bi][:, 0:512],
                          w1q[:, kc * H4 + hc * 128: kc * H4 + (hc + 1) * 128],
                          xt8[par][:, kc * 512:(kc + 1) * 512],
                          start=(kc == 0), stop=(kc == 7),
                          after=(t_wq, t_xr[st_], h16_tok[bi]))
            te = act(h16[bi][:], ps[bi][:, 0:512], AF.Relu,
                     bias=b1S[:, hc:hc + 1], after=(last, t_b1))
            tq = vcopy(h8[:, hc * 512:(hc + 1) * 512], h16[bi][:], (te,))
            h16_tok[bi] = tq
            t_h8_parts.append(tq)
            if "hpre" in taps and st_ == 0:
                tt_ = P.dma("gpsimd", tap_d["hpre"][hc * 128:(hc + 1) * 128, :],
                            h16[bi][:], after=(tq,))
                tap_dma.append(tt_)
                h16_tok[bi] = P.bar("gpsimd", (tt_,))
        t_h8 = P.bar("vector", tuple(t_h8_parts))
        h8_war[par] = t_h8
        if st_ + 1 < NST:
            build_x(st_ + 1)
        for j in range(4):
            cq = 4 * st_ + j
            jb = 4 * par + j
            last2 = []
            for nh_ in range(2):
                pt = ps[2 + nh_]
                last = None
                for hc in range(32):
                    last = mm(pt[:, 0:512],
                              h8[:, hc * 512 + j * 128: hc * 512 + (j + 1) * 128],
                              w2q[:, hc * D + nh_ * 512: hc * D + (nh_ + 1) * 512],
                              start=(hc == 0), stop=(hc == 31),
                              after=(t_h8, y_war[0]))
                last2.append(last)
            t_mm2 = P.bar("tensor", tuple(last2))
            bi = j % 2
            tml = P.dma("sync", msk[bi][:], masksc[cq * 128:(cq + 1) * 128, :],
                        after=(msk_tok[bi],))
            tq1 = vtt(y16[:, 0:512], ps[2][:, 0:512], b2bc[:, 0:512], ALU.add,
                      (t_mm2, t_b2))
            tq2 = vtt(y16[:, 512:1024], ps[3][:, 0:512], b2bc[:, 512:1024],
                      ALU.add)
            td = vtt(ydf[:], y16[:], msk[bi][:], ALU.mult, (tml,))
            msk_tok[bi] = td
            tst = ln_stats(ydf[:], lnst[:, 16:17], lnst[:, 17:18], lnst[:, 18:24],
                           after=(td,))
            ty1 = vts(yo[:], ydf[:], lnst[:, 16:17], lnst[:, 17:18],
                      ALU.subtract, ALU.mult, after=(tst,))
            ty2 = vtt(yo[:], yo[:], x2s[jb][:], ALU.add)
            tso = P.dma("scalar", yout[cq * 128:(cq + 1) * 128, :], yo[:],
                        after=(ty2,))
            t_out_dma.append(tso)
            yb = P.bar("vector", (tso,))
            y_war[0] = yb
            x2s_tok[jb] = yb

    P.bar("sync", tuple(t_out_dma))
    pf.close()
    pw.close()
    return finish_replay(nc, P, es, sems_ctx)


def _host_constants():
    tri = np.linspace(0.5, 1.5, 512)
    den = np.zeros(S, np.float64)
    for w in range(31):
        den[w * 256: w * 256 + 512] += tri
    invden_full = (1.0 / (den.astype(np.float32) + np.float32(1e-6))).astype(np.float32)
    import jax
    with jax.default_device(jax.devices("cpu")[0]):
        keep = np.asarray(jax.random.bernoulli(
            jax.random.key(42, impl="threefry2x32"), 0.8, (B, S, D)))
    maskval = np.where(keep, np.float32(1.25), np.float32(0.0))
    return tri.astype(np.float32), invden_full, maskval


def kernel(L0_attn_hidden_states, L0_attn_global_memory, L0_attn_conv_w,
           L0_attn_conv_b, L0_attn_mix_w, L0_attn_mix_b, L0_attn_out_w,
           L0_attn_out_b, L0_ffn_w1, L0_ffn_b1, L0_ffn_w2, L0_ffn_b2):
    import ml_dtypes
    x = np.ascontiguousarray(np.asarray(L0_attn_hidden_states, np.float32))
    gm = np.asarray(L0_attn_global_memory, np.float32)
    conv_w = np.asarray(L0_attn_conv_w, np.float32)
    conv_b = np.asarray(L0_attn_conv_b, np.float32)
    mix_w = np.ascontiguousarray(np.asarray(L0_attn_mix_w, np.float32))
    mix_b = np.asarray(L0_attn_mix_b, np.float32)
    out_w = np.ascontiguousarray(np.asarray(L0_attn_out_w, np.float32))
    out_b = np.asarray(L0_attn_out_b, np.float32)
    w1 = np.asarray(L0_ffn_w1, np.float32)
    b1 = np.asarray(L0_ffn_b1, np.float32)
    w2 = np.asarray(L0_ffn_w2, np.float32)
    b2v = np.asarray(L0_ffn_b2, np.float32)

    tri, invden_full, maskval = _host_constants()

    memT = np.ascontiguousarray(
        gm.transpose(0, 2, 1, 3).reshape(64, 1024)
          .reshape(64, 8, 128).transpose(2, 1, 0).reshape(128, 512))
    convwT = np.ascontiguousarray(conv_w.transpose(2, 1, 0))
    convbC = np.ascontiguousarray(conv_b.reshape(8, 128).T)
    mixbC = np.ascontiguousarray(mix_b.reshape(8, 128).T)
    w1T = np.ascontiguousarray(w1.T)
    b1C = np.ascontiguousarray(b1.reshape(32, 128).T)
    w2T = np.ascontiguousarray(w2.T)
    ident = np.eye(128, dtype=np.float32)
    identb = np.eye(128).astype(ml_dtypes.bfloat16)
    xdsT = np.ascontiguousarray(x.transpose(2, 0, 1).reshape(D, BS))

    in_maps = []
    pidx = np.arange(128)
    for c in range(NC):
        b = c // 4
        r0 = (c % 4) * TOK
        xs = np.zeros((SLAB, D), np.float32)
        lo, hi = r0 - HALO, r0 + TOK + HALO
        slo, shi = max(lo, 0), min(hi, S)
        xs[slo - lo: shi - lo] = x[b, slo:shi]
        triTv = np.zeros((128, 4 * NW), np.float32)
        W0 = r0 // 256 - 1
        for w in range(NW):
            if 0 <= W0 + w <= 30:
                for jj in range(4):
                    triTv[:, 4 * w + jj] = tri[jj * 128: jj * 128 + 128]
        invdC = np.ascontiguousarray(invden_full[r0:r0 + TOK].reshape(16, 128).T)
        bcixv = np.zeros((128, 32), np.int32)
        for st_ in range(NST):
            for dout in range(8):
                row = (b * NC + dout) * 128 + pidx
                bcixv[:, st_ * 8 + dout] = row * (S // 512) + (c % 4) * 4 + st_
        in_maps.append({
            "xslab": xs,
            "xds": np.ascontiguousarray(xdsT[c * 128:(c + 1) * 128]),
            "x0": np.ascontiguousarray(x[b, 0:256]),
            "memT": memT, "convwT": convwT, "convb": convbC,
            "mixw": mix_w, "mixb": mixbC, "outw": out_w, "outb": out_b,
            "w1T": w1T, "b1c": b1C, "w2T": w2T, "b2": b2v,
            "masksc": np.ascontiguousarray(maskval[b, r0:r0 + TOK]),
            "triT": triTv, "invden": invdC, "ident32": ident,
            "identb": identb, "bcidx": bcixv,
        })

    key = (tuple(sorted(_TAPS)), _STAGE)
    if key not in _PROG_CACHE:
        _PROG_CACHE[key] = build_program(key[0])
    nc = _PROG_CACHE[key]
    res = run_bass_kernel_spmd(nc, in_maps, core_ids=list(range(NC)))

    _TAP_RESULTS.clear()
    for tn in _TAPS:
        _TAP_RESULTS[tn] = [r["tap_" + tn] for r in res.results]

    out = np.empty((B, S, D), np.float32)
    for c in range(NC):
        b = c // 4
        r0 = (c % 4) * TOK
        out[b, r0:r0 + TOK] = res.results[c]["y"]
    return out


# version 20
# speedup vs baseline: 1.0052x; 1.0052x over previous
import numpy as np
from contextlib import ExitStack

import concourse.bass as bass
import concourse.mybir as mybir
from concourse.bass_utils import run_bass_kernel_spmd

F32 = mybir.dt.float32
F16 = mybir.dt.float16
F8 = mybir.dt.float8e4
BF16 = mybir.dt.bfloat16
I32 = mybir.dt.int32
AF = mybir.ActivationFunctionType
ALU = mybir.AluOpType
AX = mybir.AxisListType.X

NC = 8
B = 2
S = 8192
BS = B * S
D = 1024
NH = 16
H4 = 4096

TOK = 2048
HALO = 256
SLAB = TOK + 2 * HALO
NCH = SLAB // 128
OWN0 = 2
NOWN = 16
NW = 9
NST = 4

RING_HF = 6
RING_HT = 6

INV_D = 1.0 / D
EPS = 1e-5
ISQ = 0.125

ENGS = ("sync", "scalar", "vector", "tensor", "gpsimd")
CSEM = {"sync": "cy", "scalar": "cs", "vector": "cv", "tensor": "ct", "gpsimd": "cg"}
DSEM = {"sync": "dy", "scalar": "ds", "vector": "dv", "tensor": "dt", "gpsimd": "dg"}
SEMS = ("cy", "cs", "cv", "ct", "cg", "dy", "ds", "dv", "dt", "dg", "cc")

_TAPS = set()
_TAP_RESULTS = {}
_PROG_CACHE = {}
_STAGE = 6
_NO_RES_AG = False
_NO_STATS_AG = False
_NO_ACCUM = False


class Tok:
    __slots__ = ("sem", "val")

    def __init__(self, sem, val):
        self.sem = sem
        self.val = val


class Plan:
    def __init__(self):
        self.streams = {e: [] for e in ENGS}
        self.cnt = {s: 0 for s in SEMS}
        self.maxw = {e: {} for e in ENGS}
        self.last = {e: None for e in ENGS}

    def op(self, eng, emit, after=(), sem=None, n=1):
        if sem is None:
            sem = CSEM[eng]
        if eng in ("vector", "scalar"):
            after = tuple(after) + (self.last[eng],)
        waits = []
        for t in after:
            if t is None:
                continue
            if self.maxw[eng].get(t.sem, 0) >= t.val:
                continue
            self.maxw[eng][t.sem] = t.val
            waits.append((t.sem, t.val))
        self.cnt[sem] += n
        self.streams[eng].append((emit, tuple(waits), sem, n))
        tok = Tok(sem, self.cnt[sem])
        if eng in ("vector", "scalar"):
            self.last[eng] = tok
        return tok

    def dma(self, eng, out, in_, after=(), accum=False, slow=False):
        sem = DSEM[eng]
        kw = {}
        if accum:
            kw["accum_op"] = ALU.add
        if slow:
            kw["allow_slow_non_contiguous"] = True
        emit = lambda e, o=out, i=in_, k=dict(kw): e.dma_start(out=o, in_=i, **k)
        return self.op(eng, emit, after=after, sem=sem, n=16)

    def bar(self, eng, after):
        return self.op(eng, lambda e: e.nop(), after)

    def snap(self):
        return tuple(Tok(s, self.cnt[s]) for s in SEMS if self.cnt[s] > 0)

    def finalize(self):
        waited = set()
        for e, stream in self.streams.items():
            for emit, waits, sem, n in stream:
                for ws, wv in waits:
                    waited.add((ws, wv))
        new_streams = {}
        for e, stream in self.streams.items():
            out = []
            pre = {}
            for emit, waits, sem, n in stream:
                p = pre.get(sem, 0)
                if p and (sem, p) in waited and (sem, p) not in waits:
                    waits = tuple(waits) + ((sem, p),)
                pre[sem] = p + n
                out.append((emit, waits, sem, n))
            new_streams[e] = out
        self.streams = new_streams

    def replay(self, eng_name, eng, sems):
        for emit, waits, sem, n in self.streams[eng_name]:
            for ws, wv in waits:
                eng.wait_ge(sems[ws], wv)
            emit(eng).then_inc(sems[sem], n)


def _ap(t, offset, pattern):
    return bass.AP(tensor=t.tensor if isinstance(t, bass.AP) else t,
                   offset=offset, ap=pattern)


def build_program(taps=()):
    stage = _STAGE
    nc = bass.Bass("TRN2", target_bir_lowering=False)
    taps = set(taps)

    def I(name, shape, dt=F32):
        return nc.dram_tensor(name, shape, dt, kind="ExternalInput").ap()

    xslab = I("xslab", [SLAB, D])
    xds = I("xds", [128, BS])
    x0 = I("x0", [256, D])
    memT = I("memT", [128, 512])
    convwT = I("convwT", [4, D, D])
    convb = I("convb", [128, 8])
    mixw = I("mixw", [2 * D, D])
    mixb = I("mixb", [128, 8])
    outw = I("outw", [D, D])
    outb = I("outb", [D])
    w1T = I("w1T", [D, H4])
    b1c = I("b1c", [128, 32])
    w2T = I("w2T", [H4, D])
    b2 = I("b2", [D])
    masksc = I("masksc", [TOK, D])
    triT = I("triT", [128, 4 * NW])
    invden = I("invden", [128, 16])
    ident = I("ident32", [128, 128])
    identb = I("identb", [128, 128], BF16)
    bcidx = I("bcidx", [128, 32], I32)

    yout = nc.dram_tensor("y", [TOK, D], F32, kind="ExternalOutput").ap()

    stats_in = nc.dram_tensor("stats_in", [32, 128], F32).ap()
    stats_all = nc.dram_tensor("stats_all", [NC, 32, 128], F32).ap()
    res_in = [nc.dram_tensor("res_in%d" % b, [128, S], F32).ap() for b in range(B)]
    res_ab = nc.dram_tensor("res_ab", [B * NC, 128, S], F32).ap()
    mixw_hi = nc.dram_tensor("mixw_hi", [2 * D, D], BF16).ap()
    mixw_lo = nc.dram_tensor("mixw_lo", [2 * D, D], BF16).ap()
    outw_hi = nc.dram_tensor("outw_hi", [D, D], BF16).ap()
    outw_lo = nc.dram_tensor("outw_lo", [D, D], BF16).ap()
    x2d = nc.dram_tensor("x2d", [TOK, D], F32).ap()
    local_d = nc.dram_tensor("local_d", [TOK, D], F32).ap()
    glob_d = nc.dram_tensor("glob_d", [TOK, D], F32).ap()

    tap_d = {}
    for tn in taps:
        shp = {"h": [SLAB, D], "local": [TOK, D], "glob": [TOK, D],
               "bc": [8 * 128, TOK], "x2": [TOK, D], "gtT": [128, 1024],
               "hpre": [32 * 128, 512]}[tn]
        tap_d[tn] = nc.dram_tensor("tap_" + tn, shp, F32, kind="ExternalOutput").ap()
    tap_dma = []

    P = Plan()

    es = ExitStack()

    def sb(st, name, shape, dtype=F32):
        return st.enter_context(nc.sbuf_tensor(name, shape, dtype))

    idt = sb(es, "idt", [128, 128])
    idb = sb(es, "idb", [128, 128], BF16)
    stats = sb(es, "stats", [128, 64])
    trash = sb(es, "trash", [128, D])
    smm = sb(es, "smm", [128, 8])
    gtT_h = sb(es, "gtT_h", [128, 1024], BF16)
    gtT_l = sb(es, "gtT_l", [128, 1024], BF16)
    vg = sb(es, "vg", [128, 16 * 65])
    geL = [sb(es, "geL%d" % i, [128, 128]) for i in range(2)]
    triS = sb(es, "triS", [128, 4 * NW])
    invdS = sb(es, "invdS", [128, 16])
    convbS = sb(es, "convbS", [128, 8])
    mixbS = sb(es, "mixbS", [128, 8])
    outbbc = sb(es, "outbbc", [128, D])
    lnst = sb(es, "lnst", [128, 32])
    bcix = sb(es, "bcix", [128, 32], I32)
    xst = [sb(es, "xst%d" % i, [128, D]) for i in range(2)]

    ps = [es.enter_context(nc.psum_tensor("ps%d" % i, [128, 512], F32))
          for i in range(7)]
    psb = es.enter_context(nc.psum_tensor("psb", [128, 1024], BF16))

    def vcopy(dst, src, after=()):
        return P.op("vector", lambda e, d=dst, s=src: e.tensor_copy(d, s), after)

    def vmemset(dst, val, after=()):
        return P.op("vector", lambda e, d=dst, v=val: e.memset(d, v), after)

    def vts(dst, src, s1, s2, op0, op1=None, after=()):
        if op1 is None:
            return P.op("vector",
                        lambda e, d=dst, s=src, a=s1: e.tensor_scalar(d, s, a, None, op0),
                        after)
        return P.op("vector",
                    lambda e, d=dst, s=src, a=s1, b=s2: e.tensor_scalar(d, s, a, b, op0, op1),
                    after)

    def vtt(dst, a, b, op, after=()):
        return P.op("vector", lambda e, d=dst, x=a, y=b: e.tensor_tensor(d, x, y, op), after)

    def vstt(dst, in0, scal, in1, op0, op1, after=()):
        return P.op("vector",
                    lambda e, d=dst, x=in0, s=scal, y=in1: e.scalar_tensor_tensor(d, x, s, y, op0, op1),
                    after)

    def vrecip(dst, src, after=()):
        return P.op("vector", lambda e, d=dst, s=src: e.reciprocal(d, s), after)

    def vred(dst, src, after=()):
        return P.op("vector", lambda e, d=dst, s=src: e.reduce_sum(d, s, axis=AX), after)

    def act(dst, src, fn, bias=None, scale=None, accum=None, after=()):
        kw = {}
        if bias is not None:
            kw["bias"] = bias
        if scale is not None:
            kw["scale"] = scale
        if accum is not None:
            kw["accum_out"] = accum
        return P.op("scalar",
                    lambda e, d=dst, s=src, k=dict(kw): e.activation(d, s, fn, **k),
                    after)

    def mm(pt, lhsT, rhs, start, stop, after=()):
        return P.op("tensor",
                    lambda e, o=pt, l=lhsT, r=rhs, st=start, sp=stop:
                        e.matmul(o, l, r, start=st, stop=sp),
                    after)

    def tr(pt, src, after=(), bf=False, npart=128):
        iden = idb if bf else idt
        idap = iden[:] if npart == 128 else iden[0:npart, 0:npart]
        return P.op("tensor",
                    lambda e, o=pt, s=src, i=idap: e.transpose(o, s, i),
                    after)

    def rsqrt_chain(vcol, ucol, sdcol, rcol, acol, t_v):
        t1 = vts(ucol, vcol, EPS, None, ALU.add, after=(t_v,))
        t2 = act(sdcol, ucol, AF.Sqrt, after=(t1,))
        tk = vrecip(rcol, sdcol, (t2,))
        for _ in range(2):
            ta = vts(acol, rcol, rcol, None, ALU.mult, after=(tk,))
            tb = vtt(acol, acol, ucol, ALU.mult)
            tc = vts(acol, acol, -0.5, 1.5, ALU.mult, ALU.add)
            tk = vtt(rcol, rcol, acol, ALU.mult)
        return tk

    def ln_stats(xap, mcol, rcol, sc6, after=()):
        c_sum, c_sq = sc6[:, 0:1], sc6[:, 1:2]
        c_var, c_u, c_sd, c_a = sc6[:, 2:3], sc6[:, 3:4], sc6[:, 4:5], sc6[:, 5:6]
        t1 = vred(c_sum, xap, after)
        t2 = act(trash[:], xap, AF.Square, accum=c_sq, after=after)
        t3 = vts(mcol, c_sum, INV_D, None, ALU.mult)
        t4 = vts(c_var, c_sq, INV_D, None, ALU.mult, after=(t2,))
        t5 = vts(c_u, mcol, mcol, None, ALU.mult)
        t6 = vtt(c_var, c_var, c_u, ALU.subtract)
        return rsqrt_chain(c_var, c_u, c_sd, rcol, c_a, t6)

    def split(hi, lo, src, after=()):
        t1 = vcopy(hi, src, after)
        t2 = vtt(lo, src, hi, ALU.subtract)
        return t2

    t_id = P.dma("sync", idt[:], ident)
    t_idb = P.dma("sync", idb[:], identb)
    t_tri = P.dma("sync", triS[:], triT)
    t_invd = P.dma("sync", invdS[:], invden)
    t_cb = P.dma("sync", convbS[:], convb)
    t_mb = P.dma("sync", mixbS[:], mixb)
    t_ob = P.dma("sync", outbbc[:], _ap(outb, 0, [[0, 128], [1, D]]))
    t_bcix = P.dma("sync", bcix[:], bcidx)

    def epilogue():
        outs = []
        for j in range(16):
            outs.append(P.dma("scalar", yout[j * 128:(j + 1) * 128, :], trash[:],
                              after=(P.snap())))
        P.bar("sync", tuple(outs))
        return finish_replay(nc, P, es, sems_ctx)

    sems_ctx = []

    def finish_replay(nc, P, es, _sc):
        P.finalize()
        with es:
            with ExitStack() as ses:
                sems = {sn: ses.enter_context(nc.semaphore(sn)) for sn in SEMS}
                with nc.Block() as block:
                    @block.sync
                    def _(sync):
                        P.replay("sync", sync, sems)

                    @block.scalar
                    def _(scalar):
                        P.replay("scalar", scalar, sems)

                    @block.vector
                    def _(vector):
                        P.replay("vector", vector, sems)

                    @block.tensor
                    def _(tensor):
                        P.replay("tensor", tensor, sems)

                    @block.gpsimd
                    def _(gpsimd):
                        P.replay("gpsimd", gpsimd, sems)
        return nc

    if stage < 1:
        return epilogue()

    f0 = ExitStack()
    gtT = sb(f0, "gtT", [128, 1024])
    gtp = sb(f0, "gtp", [128, D])
    h0t = sb(f0, "h0t", [128, 2 * D])
    h0b = sb(f0, "h0b", [128, D], BF16)
    h0l = sb(f0, "h0l", [128, D], BF16)
    h0T_h = sb(f0, "h0T_h", [128, 2048], BF16)
    h0T_l = sb(f0, "h0T_l", [128, 2048], BF16)
    cwb = [sb(f0, "cwb%d" % i, [128, D]) for i in range(2)]
    cwh = [sb(f0, "cwh%d" % i, [128, D], BF16) for i in range(2)]
    cwl = [sb(f0, "cwl%d" % i, [128, D], BF16) for i in range(2)]

    t_memT = P.dma("sync",
                   _ap(gtT[:], 64, [[1024, 128], [128, 8], [1, 64]]),
                   memT)



    psb_tok = [None]
    t_h0_parts = []
    for i in range(2):
        tl = P.dma("sync", xst[i][:], x0[i * 128:(i + 1) * 128, :])
        tst = ln_stats(xst[i][:], stats[:, 40 + i:41 + i], stats[:, 42 + i:43 + i],
                       stats[:, 44:50], after=(tl,))
        ta = vts(h0t[:, i * D:(i + 1) * D], xst[i][:],
                 stats[:, 40 + i:41 + i], stats[:, 42 + i:43 + i],
                 ALU.subtract, ALU.mult)
        tsp = split(h0b[:], h0l[:], h0t[:, i * D:(i + 1) * D], (ta,))
        for half, srcb in ((0, h0b), (1, h0l)):
            tts = []
            for dc in range(8):
                tts.append(tr(psb[:, dc * 128:(dc + 1) * 128],
                              srcb[:, dc * 128:(dc + 1) * 128],
                              after=(tsp, t_idb, psb_tok[0]), bf=True))
            dstten = h0T_h if half == 0 else h0T_l
            dst = _ap(dstten[:], i * 128, [[2048, 128], [256, 8], [1, 128]])
            ev = vcopy(dst, psb[:], tuple(tts))
            psb_tok[0] = ev
            t_h0_parts.append(ev)
    t_h0 = P.bar("vector", tuple(t_h0_parts))

    def h0T_strided(hi, dc_in, k):
        ten = h0T_h if hi else h0T_l
        return _ap(ten[:], dc_in * 256 + k, [[2048, 128], [4, 64]])

    cw_tok = [None, None]
    conv_last = None
    n_conv = 4 * 8 * 3 * 8
    ic = 0
    for k in range(4):
        for dc_in in range(8):
            bi = (k * 8 + dc_in) % 2
            tl = P.dma("sync", cwb[bi][:], convwT[k, dc_in * 128:(dc_in + 1) * 128, :],
                       after=(cw_tok[bi],))
            th = vcopy(cwh[bi][:], cwb[bi][:], (tl,))
            tlo = vtt(cwl[bi][:], cwb[bi][:], cwh[bi][:], ALU.subtract)
            for (lh, rhi, dep) in ((cwh[bi], True, th), (cwh[bi], False, tlo),
                                   (cwl[bi], True, tlo)):
                rhs = h0T_strided(rhi, dc_in, k)
                for dg in range(8):
                    conv_last = mm(ps[5][:, dg * 64:(dg + 1) * 64],
                                   lh[:, dg * 128:(dg + 1) * 128], rhs,
                                   start=(ic == 0), stop=(ic == n_conv - 1),
                                   after=(dep, t_h0))
                    ic += 1
            cw_tok[bi] = P.bar("tensor", (conv_last,))
    t_conv = P.bar("tensor", (conv_last,))
    t_gt_parts = [act(gtT[:, dg * 128:dg * 128 + 64], ps[5][:, dg * 64:(dg + 1) * 64],
                      AF.Identity, bias=convbS[:, dg:dg + 1],
                      after=(t_conv, t_cb, t_memT)) for dg in range(8)]
    t_gt = P.bar("scalar", tuple(t_gt_parts))
    t_gts = split(gtT_h[:], gtT_l[:], gtT[:], (t_gt,))
    t_vg_ones = vmemset(_ap(vg[:], 64, [[16 * 65, 128], [65, 16]]), 1.0)
    t_gtp_parts = []
    for dg in range(8):
        pcol = (dg % 4) * 128
        tt = tr(ps[6][:, pcol:pcol + 128], gtT[:, dg * 128:(dg + 1) * 128],
                after=(t_gt, t_id, t_gtp_parts[-1] if t_gtp_parts else None))
        t_gtp_parts.append(vcopy(gtp[:, dg * 128:(dg + 1) * 128],
                                 ps[6][:, pcol:pcol + 128], (tt,)))
    t_gtp = P.bar("vector", tuple(t_gtp_parts))
    t_vg_parts = [vcopy(vg[:, h * 65:h * 65 + 64], gtp[:, h * 64:(h + 1) * 64],
                        (t_gtp, t_vg_ones)) for h in range(NH)]
    t_vg = P.bar("vector", tuple(t_vg_parts))
    t_ps6_0 = P.bar("tensor", (t_gtp,))
    if "gtT" in taps:
        tap_dma.append(P.dma("gpsimd", tap_d["gtT"], gtT[:], after=(t_gts, t_vg)))

    t_stats = [None] * NCH
    xst_tok = [None, None]

    def emit_ln1_stats(i):
        bi = i % 2
        tl = P.dma("sync", xst[bi][:], xslab[i * 128:(i + 1) * 128, :],
                   after=(xst_tok[bi],))
        t_stats[i] = ln_stats(xst[bi][:], stats[:, i:i + 1], stats[:, 20 + i:21 + i],
                              stats[:, 50:56], after=(tl,))
        xst_tok[bi] = t_stats[i]

    t_ag_stats = [None]

    def plan_stats_ag():
        after = tuple(t_stats[OWN0:OWN0 + NOWN])
        t1 = P.dma("gpsimd", _ap(stats_in, 0, [[1, 128], [128, 16]]),
                   stats[:, OWN0:OWN0 + 16], after=after, slow=True)
        t2 = P.dma("gpsimd", _ap(stats_in, 2048, [[1, 128], [128, 16]]),
                   stats[:, 20 + OWN0:20 + OWN0 + 16], after=after, slow=True)

        if _NO_STATS_AG:
            t_ag_stats[0] = P.bar("gpsimd", (t1, t2))
            return

        def emit_cc(e):
            return e.collective_compute(
                "AllGather", ALU.bypass, replica_groups=[list(range(NC))],
                ins=[stats_in.opt()], outs=[stats_all.opt()])
        t_ag_stats[0] = P.op("gpsimd", emit_cc, (t1, t2), sem="cc")

    if stage >= 2:
        for i in range(NCH):
            emit_ln1_stats(i)
        plan_stats_ag()

    wsp_tok = [cw_tok[0], cw_tok[1]]
    for (wsrc, whi, wlo_, nchunks) in ((mixw, mixw_hi, mixw_lo, 16),
                                       (outw, outw_hi, outw_lo, 8)):
        for kc in range(nchunks):
            bi = kc % 2
            tl = P.dma("sync", cwb[bi][:], wsrc[kc * 128:(kc + 1) * 128, :],
                       after=(wsp_tok[bi],))
            th = vcopy(cwh[bi][:], cwb[bi][:], (tl,))
            tlo = vtt(cwl[bi][:], cwb[bi][:], cwh[bi][:], ALU.subtract)
            t1 = P.dma("gpsimd", whi[kc * 128:(kc + 1) * 128, :], cwh[bi][:],
                       after=(th,))
            t2 = P.dma("gpsimd", wlo_[kc * 128:(kc + 1) * 128, :], cwl[bi][:],
                       after=(tlo,))
            wsp_tok[bi] = P.bar("vector", (t1, t2))
    t_wsplit = P.bar("vector", tuple(wsp_tok))

    t_front = P.snap()
    f0.close()
    if stage < 2:
        return epilogue()

    pa = ExitStack()
    hf = [sb(pa, "hf%d" % i, [128, 16 * 65]) for i in range(RING_HF)]
    hsb = sb(pa, "hsb", [128, D], BF16)
    hsl = sb(pa, "hsl", [128, D], BF16)
    hTh = sb(pa, "hTh", [128, RING_HT * D], BF16)
    hTl = sb(pa, "hTl", [128, RING_HT * D], BF16)
    expl = [sb(pa, "expl%d" % i, [128, 512]) for i in range(8)]
    avt = [sb(pa, "avt%d" % i, [128, 512]) for i in range(2)]
    fbuf = [sb(pa, "fbuf%d" % i, [128, 260]) for i in range(4)]
    gbuf = [sb(pa, "gbuf%d" % i, [128, 65]) for i in range(4)]
    lacc = [sb(pa, "lacc%d" % i, [128, D]) for i in range(4)]
    gacc = [sb(pa, "gacc%d" % i, [128, D]) for i in range(2)]
    bcur = sb(pa, "bcur", [128, S])
    bnxt = sb(pa, "bnxt", [128, S])
    bmr = sb(pa, "bmr", [128, 2048])

    tF = P.bar("sync", t_front)
    t_hfones = [vmemset(_ap(hf[i][:], 64, [[16 * 65, 128], [65, 16]]), 1.0,
                        (tF,)) for i in range(RING_HF)]

    def hf65(slot):
        return _ap(hf[slot][:], 0, [[16 * 65, 128], [65, 16], [1, 64]])

    t_happly = [None] * NCH
    t_hT = [None] * NCH
    hf_war = [t_hfones[i] for i in range(RING_HF)]
    hT_war = [None] * RING_HT
    xst_tok2 = [tF, tF]

    def emit_ln1_chunk(i):
        bi = i % 2
        tl = P.dma("sync", xst[bi][:], xslab[i * 128:(i + 1) * 128, :],
                   after=(xst_tok2[bi],))
        slot = i % RING_HF
        ta = vts(hf65(slot), xst[bi][:], stats[:, i:i + 1], stats[:, 20 + i:21 + i],
                 ALU.subtract, ALU.mult, after=(hf_war[slot], tl))
        t_happly[i] = ta
        xst_tok2[bi] = ta
        if "h" in taps:
            tt_ = P.dma("gpsimd", tap_d["h"][i * 128:(i + 1) * 128, :],
                        hf65(slot), after=(ta,))
            tap_dma.append(tt_)
            hf_war[slot] = P.bar("gpsimd", (tt_, hf_war[slot]))
        tslot = i % RING_HT
        tsp = split(hsb[:], hsl[:], hf65(slot), (ta,))
        t_parts = []
        for half, srcb in ((0, hsb), (1, hsl)):
            tts = [tr(psb[:, dc * 128:(dc + 1) * 128],
                      srcb[:, dc * 128:(dc + 1) * 128],
                      after=(tsp, t_idb, hT_war[tslot], psb_tok[0]), bf=True)
                   for dc in range(8)]
            dst = hTh if half == 0 else hTl
            ev = vcopy(dst[:, tslot * D:(tslot + 1) * D], psb[:], tuple(tts))
            psb_tok[0] = ev
            t_parts.append(ev)
        t_hT[i] = P.bar("vector", tuple(t_parts))
        hT_war[tslot] = Tok("ct", P.cnt["ct"])

    SH = [1, 2, 4, 8, 16, 32, 64, 128, 256, 512, 1024, 2048, 4096]
    t_res_ag = [None]

    class Bcast:
        def __init__(self):
            self.state = "apply"
            self.bb = 0
            self.j = 0
            self.it = 0
            self.tcur = None
            self.tmr = None
            self.tstore = [None, None]
            self.cur = bcur
            self.nxt = bnxt
            self.res_done = [None, None]
            self.pieces = []
            self.iter_toks = []
            self.done = False

        def quantum(self):
            if self.done:
                return False
            bb = self.bb
            if self.state == "apply":
                j = self.j
                rk = bb * 4 + j
                sl = slice(j * 2048, (j + 1) * 2048)
                t_x = P.dma("sync", self.cur[:, sl],
                            xds[:, bb * S + j * 2048: bb * S + (j + 1) * 2048],
                            after=(self.tmr, self.tstore[0], self.tstore[1]))
                t_m = P.dma("gpsimd", bmr[:],
                            _ap(stats_all, rk * 4096, [[0, 128], [1, 2048]]),
                            after=(t_ag_stats[0], self.tmr))
                t1 = vtt(self.cur[:, sl], self.cur[:, sl], bmr[:], ALU.subtract,
                         (t_m, t_x))
                t_r = P.dma("gpsimd", bmr[:],
                            _ap(stats_all, rk * 4096 + 2048, [[0, 128], [1, 2048]]),
                            after=(t1,))
                t2 = vtt(self.cur[:, sl], self.cur[:, sl], bmr[:], ALU.mult,
                         (t_r,))
                self.tmr = t2
                self.j += 1
                if self.j == 4:
                    self.j = 0
                    self.it = 0
                    self.tcur = t2
                    self.state = "iter"
                return True
            if not self.pieces:
                sh = SH[self.it]
                war = self.tstore[0]
                self.iter_toks = []
                CH = 2048
                if sh == 4096:
                    for a0 in (0, 4096):
                        s0 = (a0 + 4096) % S
                        for off in range(0, 4096, CH):
                            self.pieces.append((a0 + off, a0 + off, s0 + off,
                                                CH, war, False))
                else:
                    for (a0, ln0) in ((0, sh), (sh, S - 2 * sh), (S - sh, sh)):
                        m0 = (a0 - sh) % S
                        p0 = (a0 + sh) % S
                        for off in range(0, ln0, CH):
                            self.pieces.append((a0 + off, m0 + off, p0 + off,
                                                min(CH, ln0 - off), war, True))
            budget = 4096
            while self.pieces and budget > 0:
                a, am, ap_, ln, war, has_stt = self.pieces.pop(0)
                t1 = vtt(self.nxt[:, a:a + ln], self.cur[:, am:am + ln],
                         self.cur[:, ap_:ap_ + ln], ALU.add,
                         (self.tcur, war))
                if has_stt:
                    t1 = vstt(self.nxt[:, a:a + ln], self.nxt[:, a:a + ln],
                              0.5, self.cur[:, a:a + ln], ALU.mult, ALU.add,
                              (t1,))
                budget -= ln * (2 if has_stt else 1)
                self.iter_toks.append(t1)
            if self.pieces:
                return True
            tall = P.bar("vector", tuple(self.iter_toks))
            acc = (self.it > 0 and not _NO_ACCUM)
            sts = [P.dma("gpsimd",
                         res_in[bb][:, q * 2048:(q + 1) * 2048],
                         self.nxt[:, q * 2048:(q + 1) * 2048],
                         after=(tall, self.tstore[1]), accum=acc)
                   for q in range(4)]
            ts_ = P.bar("gpsimd", tuple(sts))
            self.tstore = [ts_, self.tstore[0]]
            self.tcur = tall
            self.cur, self.nxt = self.nxt, self.cur
            self.it += 1
            if self.it == len(SH):
                if _NO_RES_AG:
                    self.res_done[bb] = P.bar("gpsimd", (ts_,))
                else:
                    def emit_cc(e, b=bb):
                        return e.collective_compute(
                            "AllGather", ALU.bypass,
                            replica_groups=[list(range(NC))],
                            ins=[res_in[b].opt()],
                            outs=[_ap(res_ab, b * NC * 128 * S,
                                      [[1, NC * 128 * S]])])
                    self.res_done[bb] = P.op("gpsimd", emit_cc, (ts_,), sem="cc")
                self.bb += 1
                self.state = "apply"
                self.tmr = tall
                if self.bb == B:
                    self.done = True
            return True

        def finish(self):
            while self.quantum():
                pass
            t_res_ag[0] = P.bar("gpsimd", tuple(self.res_done))

    bc = Bcast()

    t_lacc = {}
    lacc_war = [None] * 4
    gacc_war = [None] * 2
    t_lspill = [None] * NOWN
    t_gspill = [None] * NOWN
    expl_war = [None] * 4
    av_war = [None, None]
    ps45_tok = [None, None]
    gexp_war = [None, None]
    gav_war = [None, None]
    gn_war = [None, None]
    lacc_init = {}
    ps4_tok = [None]
    ps6_tok = [t_ps6_0]
    fbuf_war = [None] * 4
    gbuf_war = [None] * 4

    def emit_window(w, filler=None):
        c0 = 2 * w
        qc0 = max(c0, OWN0)
        qc1 = min(c0 + 4, OWN0 + NOWN)
        nq = (qc1 - qc0) * 128

        def logits_exp(h):
            dc = h // 2
            prow = 64 * (h % 2)
            par = h % 2
            for kc in range(4):
                c = c0 + kc
                pt = ps[kc]
                kslot = c % RING_HT
                runs = []
                cc_ = qc0
                while cc_ < qc1:
                    r0_ = cc_
                    while (cc_ + 1 < qc1 and
                           (cc_ + 1) % RING_HT == (cc_ % RING_HT) + 1):
                        cc_ += 1
                    runs.append((r0_, cc_ + 1))
                    cc_ += 1
                last = None
                np_i = 0
                n_tot = len(runs) * 3
                for (ra, rb) in runs:
                    nrun = rb - ra
                    rslot = ra % RING_HT
                    for (lhi, rhi) in ((True, True), (True, False), (False, True)):
                        lten = hTh if lhi else hTl
                        rten = hTh if rhi else hTl
                        lhsT = lten[prow:prow + 64,
                                    kslot * D + dc * 128: kslot * D + dc * 128 + 128]
                        sl0 = rten[prow:prow + 64, 0:1]
                        rhs = bass.AP(tensor=sl0.tensor,
                                      offset=sl0.offset + rslot * D + dc * 128,
                                      ap=[list(sl0.ap[0]), [D, nrun], [1, 128]])
                        qcol = (ra - qc0) * 128
                        last = mm(pt[:, qcol:qcol + nrun * 128], lhsT, rhs,
                                  start=(np_i == 0), stop=(np_i == n_tot - 1),
                                  after=(t_hT[c], t_hT[rb - 1], expl_war[kc]))
                        np_i += 1
                expl_war[kc] = act(expl[4 * par + kc][:, 0:nq], pt[:, 0:nq],
                                   AF.Exp, scale=ISQ,
                                   after=(last, av_war[par]))

        def av(h, nq_h):
            par = h % 2
            av_last = None
            for kc in range(4):
                c = c0 + kc
                av_last = mm(ps[4 + par][0:65, 0:nq_h],
                             hf[c % RING_HF][:, h * 65:h * 65 + 65],
                             expl[4 * par + kc][:, 0:nq_h],
                             start=(kc == 0), stop=(kc == 3),
                             after=(expl_war[kc], t_happly[c],
                                    t_hfones[c % RING_HF], ps45_tok[par]))
            av_war[par] = av_last
            tev = act(avt[par][0:65, 0:nq_h], ps[4 + par][0:65, 0:nq_h],
                      AF.Identity, after=(av_last,))
            ps45_tok[par] = tev
            return tev

        def fold(h, tev):
            par = h % 2
            k4 = h % 4
            nj = qc1 - qc0
            tts = [tr(ps[6][:, 65 * j:65 * j + 65],
                      avt[par][0:65, j * 128:(j + 1) * 128],
                      after=(tev, t_id, ps6_tok[0]), npart=65)
                   for j in range(nj)]
            tevf = act(fbuf[k4][:, 0:65 * nj], ps[6][:, 0:65 * nj],
                       AF.Identity, after=tuple(tts) + (fbuf_war[k4],))
            ps6_tok[0] = tevf
            trc = vrecip(smm[:, 0:nj],
                         _ap(fbuf[k4][:], 64, [[260, 128], [65, nj]]),
                         (tevf,))
            jj0 = qc0 - c0
            tsc = vtt(smm[:, 4:4 + nj], smm[:, 0:nj],
                      triS[:, 4 * w + jj0:4 * w + jj0 + nj], ALU.mult,
                      (t_tri,))
            last = tsc
            for j in range(nj):
                cq = qc0 + j
                o = cq - OWN0
                slot = o % 4
                init_tok = lacc_init.get(cq)
                if init_tok is None:
                    init_tok = vmemset(lacc[slot][:], 0.0, (lacc_war[slot],))
                    lacc_init[cq] = init_tok
                last = vstt(lacc[slot][:, h * 64:(h + 1) * 64],
                            fbuf[k4][:, 65 * j:65 * j + 64],
                            smm[:, 4 + j:5 + j],
                            lacc[slot][:, h * 64:(h + 1) * 64],
                            ALU.mult, ALU.add,
                            after=(init_tok, t_lacc.get((cq, h))))
                t_lacc[(cq, h)] = last
            fbuf_war[k4] = last

        pend = {}
        for h in range(NH + 2):
            if h < NH:
                logits_exp(h)
            if 1 <= h <= NH:
                pend[h - 1] = av(h - 1, nq)
            if h >= 2:
                fold(h - 2, pend.pop(h - 2))
            if h >= 2 and filler is not None:
                filler()

    def spill_local(c):
        o = c - OWN0
        slot = o % 4
        after = tuple(t_lacc.get((c, h)) for h in range(NH))
        tl = vts(lacc[slot][:], lacc[slot][:], invdS[:, o:o + 1], None,
                 ALU.mult, after=after + (t_invd,))
        dst = tap_d["local"] if "local" in taps else local_d
        td = P.dma("gpsimd", dst[o * 128:(o + 1) * 128, :], lacc[slot][:],
                   after=(tl,))
        t_lspill[o] = td
        lacc_war[slot] = td

    def emit_glob(c):
        o = c - OWN0
        slot = o % 2
        t_init = vmemset(gacc[slot][:], 0.0, (gacc_war[slot],))
        kslot = c % RING_HT

        def glogits(h):
            par = h % 2
            dc = h // 2
            prow = 64 * (h % 2)
            last = None
            for pi, (lhi, rhi) in enumerate(((True, True), (True, False),
                                             (False, True))):
                lten = gtT_h if lhi else gtT_l
                rten = hTh if rhi else hTl
                last = mm(ps[par][:, 0:128],
                          lten[prow:prow + 64, dc * 128:(dc + 1) * 128],
                          rten[prow:prow + 64,
                               kslot * D + dc * 128: kslot * D + dc * 128 + 128],
                          start=(pi == 0), stop=(pi == 2),
                          after=(t_gts, t_hT[c], gexp_war[par]))
            gexp_war[par] = act(geL[par][:], ps[par][:, 0:128], AF.Exp,
                                scale=ISQ, after=(last, gav_war[par]))

        def gav(h):
            par = h % 2
            k4 = h % 4
            tav = mm(ps[2 + par][:, 0:65], geL[par][:],
                     vg[:, h * 65:h * 65 + 65],
                     start=True, stop=True,
                     after=(gexp_war[par], t_vg, gn_war[par]))
            gav_war[par] = tav
            tevg = act(gbuf[k4][:], ps[2 + par][:, 0:65], AF.Identity,
                       after=(tav, gbuf_war[k4]))
            gn_war[par] = tevg
            trc = vrecip(smm[:, 2:3], gbuf[k4][:, 64:65], (tevg,))
            tn = vts(gacc[slot][:, h * 64:(h + 1) * 64], gbuf[k4][:, 0:64],
                     smm[:, 2:3], None, ALU.mult, after=(t_init,))
            gbuf_war[k4] = tn
            return tn

        tn = None
        for h in range(NH + 1):
            if h < NH:
                glogits(h)
            if h >= 1:
                tn = gav(h - 1)
        expl_war[0] = gexp_war[0]
        expl_war[1] = gexp_war[1]
        expl_war[2] = gn_war[0]
        expl_war[3] = gn_war[1]
        dst = tap_d["glob"] if "glob" in taps else glob_d
        td = P.dma("gpsimd", dst[o * 128:(o + 1) * 128, :], gacc[slot][:],
                   after=(tn,))
        t_gspill[o] = td
        gacc_war[slot] = td

    if stage >= 3:
        for i in range(NCH + 2):
            if i < NCH:
                emit_ln1_chunk(i)
            if i >= 5 and i % 2 == 1:
                w = (i - 5) // 2
                filler = (lambda: bc.quantum()) if stage >= 4 else None
                emit_window(w, filler)
                for c in (2 * w, 2 * w + 1):
                    if OWN0 <= c < OWN0 + NOWN:
                        spill_local(c)
                        emit_glob(c)
    if stage >= 4:
        bc.finish()

    if stage < 5:
        pa.close()
        return epilogue()
    t_phaseA = P.snap()
    pa.close()

    pw = ExitStack()
    w1q = sb(pw, "w1q", [128, 8 * H4], F8)
    t_w1q = [P.dma("gpsimd", _ap(w1q[:], 0, [[8 * H4, 128], [H4, 8], [1, H4]]),
                   _ap(w1T, 0, [[H4, 128], [128 * H4, 8], [1, H4]]),
                   after=t_phaseA)]

    pb = ExitStack()
    lst = sb(pb, "lst", [128, 512])
    gst = sb(pb, "gst", [128, 512])
    mixwR_h = sb(pb, "mixwR_h", [128, 16 * D], BF16)
    outwR_h = sb(pb, "outwR_h", [128, 8 * D], BF16)
    wlo = [sb(pb, "wlo%d" % i, [128, D], BF16) for i in range(2)]
    catTh = sb(pb, "catTh", [128, 16 * 512], BF16)
    catTl = sb(pb, "catTl", [128, 16 * 512], BF16)
    gateT = sb(pb, "gateT", [128, 512])
    lT = sb(pb, "lT", [128, 512])
    gT = sb(pb, "gT", [128, 512])
    bcT = sb(pb, "bcT", [128, 512])
    mixT = sb(pb, "mixT", [128, 512])
    mixTh = sb(pb, "mixTh", [128, 8 * 512], BF16)
    mixTl = sb(pb, "mixTl", [128, 8 * 512], BF16)
    aof = sb(pb, "aof", [128, D])
    x2c = sb(pb, "x2c", [128, D])
    xrs = sb(pb, "xrs", [128, D])

    local_src = tap_d["local"] if "local" in taps else local_d
    glob_src = tap_d["glob"] if "glob" in taps else glob_d

    tA = P.bar("sync", t_phaseA)
    t_mwr = P.dma("sync", mixwR_h[:],
                  _ap(mixw_hi, 0, [[D, 128], [128 * D, 16], [1, D]]), after=(tA,))
    t_owr = P.dma("sync", outwR_h[:],
                  _ap(outw_hi, 0, [[D, 128], [128 * D, 8], [1, D]]), after=(tA,))
    t_x2st = [None] * NST
    mixed_war = [None]
    gate_war = [None]
    x2c_war = [tA]
    xrs_war = [tA]
    lg_tok = [tA, tA]
    wlo_tok = [tA, tA]
    ps67_tok = [ps6_tok[0]]
    cat_war = [None]

    def load_lg(buf, src_d, o0, dcc, after):
        src = _ap(src_d, o0 * 128 * D + dcc * 128,
                  [[D, 128], [128 * D, 4], [1, 128]])
        return P.dma("sync", buf[:], src, after=after)

    def emit_supertile(st_):
        ch0 = OWN0 + 4 * st_
        o0 = 4 * st_
        spills = tuple(t_lspill[o0:o0 + 4] + t_gspill[o0:o0 + 4])
        t_cat = []
        lst2 = [lst, gst]
        for kc in range(16):
            src_d = local_src if kc < 8 else glob_src
            dcc = kc % 8
            par = kc % 2
            pt = ps[4 + par]
            tld = load_lg(lst2[par], src_d, o0, dcc, spills + (lg_tok[par],))
            tt_list = [tr(pt[:, j * 128:(j + 1) * 128],
                          lst2[par][:, j * 128:(j + 1) * 128],
                          after=(tld, t_id, ps45_tok[par]))
                       for j in range(4)]
            sl = slice(kc * 512, (kc + 1) * 512)
            th = vcopy(catTh[:, sl], pt[:, 0:512], tuple(tt_list) + (cat_war[0],))
            tl2 = vtt(catTl[:, sl], pt[:, 0:512], catTh[:, sl], ALU.subtract)
            ps45_tok[par] = tl2
            lg_tok[par] = P.bar("tensor", (tl2,))
            t_cat.append(tl2)
        t_catall = P.bar("vector", tuple(t_cat))
        t_split_done = []
        for grp in range(2):
            gate_last = None
            mm_all = []
            for kc in range(16):
                bi = kc % 2
                tlo = P.dma("sync", wlo[bi][:],
                            mixw_lo[kc * 128:(kc + 1) * 128, :],
                            after=(wlo_tok[bi],))
                sl = slice(kc * 512, (kc + 1) * 512)
                mmts = []
                for dg in range(4):
                    dout = grp * 4 + dg
                    wcol = kc * D + dout * 128
                    for p_, (lt_, rt_) in enumerate((
                            (mixwR_h[:, wcol:wcol + 128], catTh[:, sl]),
                            (mixwR_h[:, wcol:wcol + 128], catTl[:, sl]),
                            (wlo[bi][:, dout * 128:(dout + 1) * 128],
                             catTh[:, sl]))):
                        gate_last = mm(
                            ps[dg][:, 0:512], lt_, rt_,
                            start=(kc == 0 and p_ == 0),
                            stop=(kc == 15 and p_ == 2),
                            after=(t_mwr, tlo, t_catall) + tuple(t_split_done))
                        mmts.append(gate_last)
                mm_all.extend(mmts)
                wlo_tok[bi] = P.bar("tensor", (gate_last,))
            t_gmm = P.bar("tensor", (gate_last,))
            for dg in range(4):
                dout = grp * 4 + dg
                tg = act(gateT[:], ps[dg][:, 0:512], AF.Sigmoid,
                         bias=mixbS[:, dout:dout + 1],
                         after=(t_gmm, t_mb, gate_war[0]))
                lsl = slice(dout * 512, (dout + 1) * 512)
                gsl = slice((8 + dout) * 512, (9 + dout) * 512)
                te1 = vtt(lT[:], catTh[:, lsl], catTl[:, lsl], ALU.add, (t_catall,))
                te2 = vtt(gT[:], catTh[:, gsl], catTl[:, gsl], ALU.add)
                ixcol = st_ * 8 + dout
                tbc = P.op(
                    "gpsimd",
                    lambda e, col=ixcol: e.indirect_dma_start(
                        out=bcT[:], out_offset=None,
                        in_=_ap(res_ab, 0, [[512, B * NC * 128 * (S // 512)], [1, 512]]),
                        in_offset=bass.IndirectOffsetOnAxis(
                            ap=bcix[:, col:col + 1], axis=0)),
                    after=(t_res_ag[0], t_bcix, mixed_war[0]), sem="dg", n=16)
                tm1 = vtt(mixT[:], lT[:], gT[:], ALU.subtract)
                tm2 = vtt(mixT[:], gateT[:], mixT[:], ALU.mult, (tg,))
                gate_war[0] = tm2
                tm3 = vtt(mixT[:], mixT[:], gT[:], ALU.add)
                tm4 = vstt(mixT[:], bcT[:], 1.0 / 14.0, mixT[:], ALU.mult,
                           ALU.add, (tbc,))
                mixed_war[0] = tm4
                tsp2 = split(mixTh[:, dout * 512:(dout + 1) * 512],
                             mixTl[:, dout * 512:(dout + 1) * 512], mixT[:])
                t_split_done.append(tsp2)
                if "bc" in taps:
                    tap_dma.append(P.dma("gpsimd",
                                         tap_d["bc"][dout * 128:(dout + 1) * 128,
                                                     st_ * 512:(st_ + 1) * 512],
                                         bcT[:], after=(tm4,)))
        cat_war[0] = t_split_done[-1]
        t_mix = P.bar("vector", tuple(t_split_done))
        x2toks = []
        for j in range(4):
            cq = ch0 + j
            last = None
            for kc in range(8):
                bi = kc % 2
                tlo = P.dma("sync", wlo[bi][:],
                            outw_lo[kc * 128:(kc + 1) * 128, :],
                            after=(wlo_tok[bi],))
                for nh_ in range(2):
                    pt = ps[5 + nh_]
                    for p_, lt_ in enumerate((mixTh, mixTl, mixTh)):
                        lhsT = lt_[:, kc * 512 + j * 128: kc * 512 + (j + 1) * 128]
                        rhs = (wlo[bi][:, nh_ * 512:(nh_ + 1) * 512] if p_ == 2
                               else outwR_h[:, kc * D + nh_ * 512: kc * D + (nh_ + 1) * 512])
                        last = mm(pt[:, 0:512], lhsT, rhs,
                                  start=(kc == 0 and p_ == 0),
                                  stop=(kc == 7 and p_ == 2),
                                  after=(t_owr, tlo, t_mix, ps67_tok[0],
                                         ps45_tok[1]))
                wlo_tok[bi] = P.bar("tensor", (last,))
            t_ao = P.bar("tensor", (last,))
            ta1 = vtt(aof[:, 0:512], ps[5][:, 0:512], outbbc[:, 0:512], ALU.add,
                      (t_ao, t_ob))
            ta2 = vtt(aof[:, 512:1024], ps[6][:, 0:512], outbbc[:, 512:1024],
                      ALU.add)
            ps67_tok[0] = ta2
            tst = ln_stats(aof[:], lnst[:, 0:1], lnst[:, 1:2], lnst[:, 2:8],
                           after=(ta2,))
            trs = P.dma("sync", xrs[:], xslab[cq * 128:(cq + 1) * 128, :],
                        after=(xrs_war[0],))
            tt1 = vts(x2c[:], aof[:], lnst[:, 0:1], lnst[:, 1:2],
                      ALU.subtract, ALU.mult, after=(tst, x2c_war[0]))
            tt2 = vtt(x2c[:], x2c[:], xrs[:], ALU.add, (trs,))
            xrs_war[0] = tt2
            tso = P.dma("gpsimd", x2d[(cq - OWN0) * 128:(cq - OWN0 + 1) * 128, :],
                        x2c[:], after=(tt2,))
            x2c_war[0] = tso
            x2toks.append(tso)
            if "x2" in taps:
                tap_dma.append(P.dma("gpsimd",
                                     tap_d["x2"][(cq - OWN0) * 128:(cq - OWN0 + 1) * 128, :],
                                     x2c[:], after=(tso,)))
        t_x2st[st_] = P.bar("gpsimd", tuple(x2toks))

    for st_ in range(NST):
        emit_supertile(st_)

    t_phaseB = P.snap()
    pb.close()
    if stage < 6:
        return epilogue()

    pf = ExitStack()
    w2q = sb(pf, "w2q", [128, 32 * D], F8)
    b1S = sb(pf, "b1S", [128, 32])
    b1q8 = sb(pf, "b1q8", [128, 32], F8)
    b2bc = sb(pf, "b2bc", [128, D])
    b2q8 = sb(pf, "b2q8", [128, D], F8)
    x2s = [sb(pf, "x2s%d" % i, [128, D]) for i in range(8)]
    xf = sb(pf, "xf", [128, D])
    xt8 = [sb(pf, "xt8_%d" % i, [128, 8 * 512], F8) for i in range(2)]
    h16 = [sb(pf, "h16_%d" % i, [128, 512], F16) for i in range(2)]
    h8 = sb(pf, "h8", [128, 32 * 512], F8)
    y16 = sb(pf, "y16", [128, D], F16)
    ydf = sb(pf, "ydf", [128, D])
    msk = [sb(pf, "msk%d" % i, [128, D]) for i in range(2)]
    yo = sb(pf, "yo", [128, D])

    pA = t_phaseB
    t_b1l = P.dma("sync", b1S[:], b1c, after=pA)
    tb1a = vcopy(b1q8[:], b1S[:], (t_b1l,))
    t_b1 = vcopy(b1S[:], b1q8[:])
    t_b2l = P.dma("sync", b2bc[:], _ap(b2, 0, [[0, 128], [1, D]]), after=pA)
    tb2a = vcopy(b2q8[:], b2bc[:], (t_b2l,))
    t_b2 = vcopy(b2bc[:], b2q8[:])

    t_w2q = P.dma("gpsimd", _ap(w2q[:], 0, [[32 * D, 128], [D, 32], [1, D]]),
                  _ap(w2T, 0, [[D, 128], [128 * D, 32], [1, D]]),
                  after=pA)
    t_wq = P.bar("gpsimd", (t_w1q[0], t_w2q))

    t_out_dma = list(tap_dma)
    msk_tok = [None, None]
    x2s_tok = [None] * 8
    h16_tok = [Tok("ct", P.cnt["ct"]), Tok("ct", P.cnt["ct"])]
    xf_war = [None]
    y_war = [None]
    ps4_tok_f = [ps45_tok[0], ps45_tok[1]]
    h8_war = [None, None]
    t_xr = [None] * NST

    def build_x(st_):
        par = st_ % 2
        txt = []
        for j in range(4):
            cq = 4 * st_ + j
            jb = 4 * par + j
            tl = P.dma("sync", x2s[jb][:], x2d[cq * 128:(cq + 1) * 128, :],
                       after=(t_x2st[st_], x2s_tok[jb]))
            tst = ln_stats(x2s[jb][:], lnst[:, 8:9], lnst[:, 9:10], lnst[:, 10:16],
                           after=(tl,))
            ta = vts(xf[:], x2s[jb][:], lnst[:, 8:9], lnst[:, 9:10],
                     ALU.subtract, ALU.mult, after=(tst, xf_war[0]))
            parts = []
            for dg4 in range(2):
                pt = ps[4 + dg4]
                tts = [tr(pt[:, dc4 * 128:(dc4 + 1) * 128],
                          xf[:, (dg4 * 4 + dc4) * 128:(dg4 * 4 + dc4 + 1) * 128],
                          after=(ta, t_id, ps4_tok_f[dg4]))
                       for dc4 in range(4)]
                dst = _ap(xt8[par][:], (dg4 * 4) * 512 + j * 128,
                          [[8 * 512, 128], [512, 4], [1, 128]])
                tq = vcopy(dst, pt[:, 0:512], tuple(tts) + (h8_war[par],))
                ps4_tok_f[dg4] = tq
                parts.append(tq)
            txt.append(P.bar("vector", tuple(parts)))
            xf_war[0] = txt[-1]
        t_xr[st_] = P.bar("vector", tuple(txt))

    build_x(0)
    for st_ in range(NST):
        par = st_ % 2
        t_h8_parts = []
        for hc in range(32):
            bi = hc % 2
            last = None
            for kc in range(8):
                last = mm(ps[bi][:, 0:512],
                          w1q[:, kc * H4 + hc * 128: kc * H4 + (hc + 1) * 128],
                          xt8[par][:, kc * 512:(kc + 1) * 512],
                          start=(kc == 0), stop=(kc == 7),
                          after=(t_wq, t_xr[st_], h16_tok[bi]))
            te = act(h16[bi][:], ps[bi][:, 0:512], AF.Relu,
                     bias=b1S[:, hc:hc + 1], after=(last, t_b1))
            tq = vcopy(h8[:, hc * 512:(hc + 1) * 512], h16[bi][:], (te,))
            h16_tok[bi] = tq
            t_h8_parts.append(tq)
            if "hpre" in taps and st_ == 0:
                tt_ = P.dma("gpsimd", tap_d["hpre"][hc * 128:(hc + 1) * 128, :],
                            h16[bi][:], after=(tq,))
                tap_dma.append(tt_)
                h16_tok[bi] = P.bar("gpsimd", (tt_,))
        t_h8 = P.bar("vector", tuple(t_h8_parts))
        h8_war[par] = t_h8
        if st_ + 1 < NST:
            build_x(st_ + 1)
        for j in range(4):
            cq = 4 * st_ + j
            jb = 4 * par + j
            last2 = []
            for nh_ in range(2):
                pt = ps[2 + nh_]
                last = None
                for hc in range(32):
                    last = mm(pt[:, 0:512],
                              h8[:, hc * 512 + j * 128: hc * 512 + (j + 1) * 128],
                              w2q[:, hc * D + nh_ * 512: hc * D + (nh_ + 1) * 512],
                              start=(hc == 0), stop=(hc == 31),
                              after=(t_h8, y_war[0]))
                last2.append(last)
            t_mm2 = P.bar("tensor", tuple(last2))
            bi = j % 2
            tml = P.dma("sync", msk[bi][:], masksc[cq * 128:(cq + 1) * 128, :],
                        after=(msk_tok[bi],))
            tq1 = vtt(y16[:, 0:512], ps[2][:, 0:512], b2bc[:, 0:512], ALU.add,
                      (t_mm2, t_b2))
            tq2 = vtt(y16[:, 512:1024], ps[3][:, 0:512], b2bc[:, 512:1024],
                      ALU.add)
            td = vtt(ydf[:], y16[:], msk[bi][:], ALU.mult, (tml,))
            msk_tok[bi] = td
            tst = ln_stats(ydf[:], lnst[:, 16:17], lnst[:, 17:18], lnst[:, 18:24],
                           after=(td,))
            ty1 = vts(yo[:], ydf[:], lnst[:, 16:17], lnst[:, 17:18],
                      ALU.subtract, ALU.mult, after=(tst,))
            ty2 = vtt(yo[:], yo[:], x2s[jb][:], ALU.add)
            tso = P.dma("scalar", yout[cq * 128:(cq + 1) * 128, :], yo[:],
                        after=(ty2,))
            t_out_dma.append(tso)
            yb = P.bar("vector", (tso,))
            y_war[0] = yb
            x2s_tok[jb] = yb

    P.bar("sync", tuple(t_out_dma))
    pf.close()
    pw.close()
    return finish_replay(nc, P, es, sems_ctx)


def _host_constants():
    tri = np.linspace(0.5, 1.5, 512)
    den = np.zeros(S, np.float64)
    for w in range(31):
        den[w * 256: w * 256 + 512] += tri
    invden_full = (1.0 / (den.astype(np.float32) + np.float32(1e-6))).astype(np.float32)
    import jax
    with jax.default_device(jax.devices("cpu")[0]):
        keep = np.asarray(jax.random.bernoulli(
            jax.random.key(42, impl="threefry2x32"), 0.8, (B, S, D)))
    maskval = np.where(keep, np.float32(1.25), np.float32(0.0))
    return tri.astype(np.float32), invden_full, maskval


def kernel(L0_attn_hidden_states, L0_attn_global_memory, L0_attn_conv_w,
           L0_attn_conv_b, L0_attn_mix_w, L0_attn_mix_b, L0_attn_out_w,
           L0_attn_out_b, L0_ffn_w1, L0_ffn_b1, L0_ffn_w2, L0_ffn_b2):
    import ml_dtypes
    x = np.ascontiguousarray(np.asarray(L0_attn_hidden_states, np.float32))
    gm = np.asarray(L0_attn_global_memory, np.float32)
    conv_w = np.asarray(L0_attn_conv_w, np.float32)
    conv_b = np.asarray(L0_attn_conv_b, np.float32)
    mix_w = np.ascontiguousarray(np.asarray(L0_attn_mix_w, np.float32))
    mix_b = np.asarray(L0_attn_mix_b, np.float32)
    out_w = np.ascontiguousarray(np.asarray(L0_attn_out_w, np.float32))
    out_b = np.asarray(L0_attn_out_b, np.float32)
    w1 = np.asarray(L0_ffn_w1, np.float32)
    b1 = np.asarray(L0_ffn_b1, np.float32)
    w2 = np.asarray(L0_ffn_w2, np.float32)
    b2v = np.asarray(L0_ffn_b2, np.float32)

    tri, invden_full, maskval = _host_constants()

    memT = np.ascontiguousarray(
        gm.transpose(0, 2, 1, 3).reshape(64, 1024)
          .reshape(64, 8, 128).transpose(2, 1, 0).reshape(128, 512))
    convwT = np.ascontiguousarray(conv_w.transpose(2, 1, 0))
    convbC = np.ascontiguousarray(conv_b.reshape(8, 128).T)
    mixbC = np.ascontiguousarray(mix_b.reshape(8, 128).T)
    w1T = np.ascontiguousarray(w1.T)
    b1C = np.ascontiguousarray(b1.reshape(32, 128).T)
    w2T = np.ascontiguousarray(w2.T)
    ident = np.eye(128, dtype=np.float32)
    identb = np.eye(128).astype(ml_dtypes.bfloat16)
    xdsT = np.ascontiguousarray(x.transpose(2, 0, 1).reshape(D, BS))

    in_maps = []
    pidx = np.arange(128)
    for c in range(NC):
        b = c // 4
        r0 = (c % 4) * TOK
        xs = np.zeros((SLAB, D), np.float32)
        lo, hi = r0 - HALO, r0 + TOK + HALO
        slo, shi = max(lo, 0), min(hi, S)
        xs[slo - lo: shi - lo] = x[b, slo:shi]
        triTv = np.zeros((128, 4 * NW), np.float32)
        W0 = r0 // 256 - 1
        for w in range(NW):
            if 0 <= W0 + w <= 30:
                for jj in range(4):
                    triTv[:, 4 * w + jj] = tri[jj * 128: jj * 128 + 128]
        invdC = np.ascontiguousarray(invden_full[r0:r0 + TOK].reshape(16, 128).T)
        bcixv = np.zeros((128, 32), np.int32)
        for st_ in range(NST):
            for dout in range(8):
                row = (b * NC + dout) * 128 + pidx
                bcixv[:, st_ * 8 + dout] = row * (S // 512) + (c % 4) * 4 + st_
        in_maps.append({
            "xslab": xs,
            "xds": np.ascontiguousarray(xdsT[c * 128:(c + 1) * 128]),
            "x0": np.ascontiguousarray(x[b, 0:256]),
            "memT": memT, "convwT": convwT, "convb": convbC,
            "mixw": mix_w, "mixb": mixbC, "outw": out_w, "outb": out_b,
            "w1T": w1T, "b1c": b1C, "w2T": w2T, "b2": b2v,
            "masksc": np.ascontiguousarray(maskval[b, r0:r0 + TOK]),
            "triT": triTv, "invden": invdC, "ident32": ident,
            "identb": identb, "bcidx": bcixv,
        })

    key = (tuple(sorted(_TAPS)), _STAGE)
    if key not in _PROG_CACHE:
        _PROG_CACHE[key] = build_program(key[0])
    nc = _PROG_CACHE[key]
    res = run_bass_kernel_spmd(nc, in_maps, core_ids=list(range(NC)))

    _TAP_RESULTS.clear()
    for tn in _TAPS:
        _TAP_RESULTS[tn] = [r["tap_" + tn] for r in res.results]

    out = np.empty((B, S, D), np.float32)
    for c in range(NC):
        b = c // 4
        r0 = (c % 4) * TOK
        out[b, r0:r0 + TOK] = res.results[c]["y"]
    return out


# version 21 (final)
# speedup vs baseline: 1.0566x; 1.0511x over previous
import numpy as np
from contextlib import ExitStack

import concourse.bass as bass
import concourse.mybir as mybir
from concourse.bass_utils import run_bass_kernel_spmd

F32 = mybir.dt.float32
F16 = mybir.dt.float16
F8 = mybir.dt.float8e4
BF16 = mybir.dt.bfloat16
I32 = mybir.dt.int32
AF = mybir.ActivationFunctionType
ALU = mybir.AluOpType
AX = mybir.AxisListType.X

NC = 8
B = 2
S = 8192
BS = B * S
D = 1024
NH = 16
H4 = 4096

TOK = 2048
HALO = 256
SLAB = TOK + 2 * HALO
NCH = SLAB // 128
OWN0 = 2
NOWN = 16
NW = 9
NST = 4

RING_HF = 6
RING_HT = 6

INV_D = 1.0 / D
EPS = 1e-5
ISQ = 0.125

ENGS = ("sync", "scalar", "vector", "tensor", "gpsimd")
CSEM = {"sync": "cy", "scalar": "cs", "vector": "cv", "tensor": "ct", "gpsimd": "cg"}
DSEM = {"sync": "dy", "scalar": "ds", "vector": "dv", "tensor": "dt", "gpsimd": "dg"}
SEMS = ("cy", "cs", "cv", "ct", "cg", "dy", "ds", "dv", "dt", "dg", "cc")

_TAPS = set()
_TAP_RESULTS = {}
_PROG_CACHE = {}
_STAGE = 6
_NO_RES_AG = False
_NO_STATS_AG = False
_NO_ACCUM = False


class Tok:
    __slots__ = ("sem", "val")

    def __init__(self, sem, val):
        self.sem = sem
        self.val = val


class Plan:
    def __init__(self):
        self.streams = {e: [] for e in ENGS}
        self.cnt = {s: 0 for s in SEMS}
        self.maxw = {e: {} for e in ENGS}
        self.last = {e: None for e in ENGS}

    def op(self, eng, emit, after=(), sem=None, n=1):
        if sem is None:
            sem = CSEM[eng]
        if eng in ("vector", "scalar"):
            after = tuple(after) + (self.last[eng],)
        waits = []
        for t in after:
            if t is None:
                continue
            if self.maxw[eng].get(t.sem, 0) >= t.val:
                continue
            self.maxw[eng][t.sem] = t.val
            waits.append((t.sem, t.val))
        self.cnt[sem] += n
        self.streams[eng].append((emit, tuple(waits), sem, n))
        tok = Tok(sem, self.cnt[sem])
        if eng in ("vector", "scalar"):
            self.last[eng] = tok
        return tok

    def dma(self, eng, out, in_, after=(), accum=False, slow=False):
        sem = DSEM[eng]
        kw = {}
        if accum:
            kw["accum_op"] = ALU.add
        if slow:
            kw["allow_slow_non_contiguous"] = True
        emit = lambda e, o=out, i=in_, k=dict(kw): e.dma_start(out=o, in_=i, **k)
        return self.op(eng, emit, after=after, sem=sem, n=16)

    def bar(self, eng, after):
        return self.op(eng, lambda e: e.nop(), after)

    def snap(self):
        return tuple(Tok(s, self.cnt[s]) for s in SEMS if self.cnt[s] > 0)

    def finalize(self):
        waited = set()
        for e, stream in self.streams.items():
            for emit, waits, sem, n in stream:
                for ws, wv in waits:
                    waited.add((ws, wv))
        new_streams = {}
        for e, stream in self.streams.items():
            out = []
            pre = {}
            for emit, waits, sem, n in stream:
                p = pre.get(sem, 0)
                if p and (sem, p) in waited and (sem, p) not in waits:
                    waits = tuple(waits) + ((sem, p),)
                pre[sem] = p + n
                out.append((emit, waits, sem, n))
            new_streams[e] = out
        self.streams = new_streams

    def replay(self, eng_name, eng, sems):
        for emit, waits, sem, n in self.streams[eng_name]:
            for ws, wv in waits:
                eng.wait_ge(sems[ws], wv)
            emit(eng).then_inc(sems[sem], n)


def _ap(t, offset, pattern):
    return bass.AP(tensor=t.tensor if isinstance(t, bass.AP) else t,
                   offset=offset, ap=pattern)


def build_program(taps=()):
    stage = _STAGE
    nc = bass.Bass("TRN2", target_bir_lowering=False)
    taps = set(taps)

    def I(name, shape, dt=F32):
        return nc.dram_tensor(name, shape, dt, kind="ExternalInput").ap()

    xslab = I("xslab", [SLAB, D])
    xds = I("xds", [128, BS])
    x0 = I("x0", [256, D])
    memT = I("memT", [128, 512])
    convwT = I("convwT", [4, D, D])
    convb = I("convb", [128, 8])
    mixw = I("mixw", [2 * D, D])
    mixb = I("mixb", [128, 8])
    outw = I("outw", [D, D])
    outb = I("outb", [D])
    w1T = I("w1T", [D, H4])
    b1c = I("b1c", [128, 32])
    w2T = I("w2T", [H4, D])
    b2 = I("b2", [D])
    masksc = I("masksc", [TOK, D])
    triT = I("triT", [128, 4 * NW])
    invden = I("invden", [128, 16])
    ident = I("ident32", [128, 128])
    identb = I("identb", [128, 128], BF16)
    bcidx = I("bcidx", [128, 32], I32)

    yout = nc.dram_tensor("y", [TOK, D], F32, kind="ExternalOutput").ap()

    stats_in = nc.dram_tensor("stats_in", [32, 128], F32).ap()
    stats_all = nc.dram_tensor("stats_all", [NC, 32, 128], F32).ap()
    res_in = [nc.dram_tensor("res_in%d" % b, [128, S], F32).ap() for b in range(B)]
    res_ab = nc.dram_tensor("res_ab", [B * NC, 128, S], F32).ap()
    mixw_hi = nc.dram_tensor("mixw_hi", [2 * D, D], BF16).ap()
    mixw_lo = nc.dram_tensor("mixw_lo", [2 * D, D], BF16).ap()
    outw_hi = nc.dram_tensor("outw_hi", [D, D], BF16).ap()
    outw_lo = nc.dram_tensor("outw_lo", [D, D], BF16).ap()
    x2d = nc.dram_tensor("x2d", [TOK, D], F32).ap()
    local_d = nc.dram_tensor("local_d", [TOK, D], F32).ap()
    glob_d = nc.dram_tensor("glob_d", [TOK, D], F32).ap()

    tap_d = {}
    for tn in taps:
        shp = {"h": [SLAB, D], "local": [TOK, D], "glob": [TOK, D],
               "bc": [8 * 128, TOK], "x2": [TOK, D], "gtT": [128, 1024],
               "hpre": [32 * 128, 512]}[tn]
        tap_d[tn] = nc.dram_tensor("tap_" + tn, shp, F32, kind="ExternalOutput").ap()
    tap_dma = []

    P = Plan()

    es = ExitStack()

    def sb(st, name, shape, dtype=F32):
        return st.enter_context(nc.sbuf_tensor(name, shape, dtype))

    idt = sb(es, "idt", [128, 128])
    idb = sb(es, "idb", [128, 128], BF16)
    stats = sb(es, "stats", [128, 64])
    trash = sb(es, "trash", [128, D])
    smm = sb(es, "smm", [128, 8])
    gtT_h = sb(es, "gtT_h", [128, 1024], BF16)
    gtT_l = sb(es, "gtT_l", [128, 1024], BF16)
    vg = sb(es, "vg", [128, 16 * 65])
    geL = [sb(es, "geL%d" % i, [128, 128]) for i in range(2)]
    triS = sb(es, "triS", [128, 4 * NW])
    invdS = sb(es, "invdS", [128, 16])
    convbS = sb(es, "convbS", [128, 8])
    mixbS = sb(es, "mixbS", [128, 8])
    outbbc = sb(es, "outbbc", [128, D])
    lnst = sb(es, "lnst", [128, 32])
    bcix = sb(es, "bcix", [128, 32], I32)
    xst = [sb(es, "xst%d" % i, [128, D]) for i in range(2)]

    ps = [es.enter_context(nc.psum_tensor("ps%d" % i, [128, 512], F32))
          for i in range(7)]
    psb = es.enter_context(nc.psum_tensor("psb", [128, 1024], BF16))

    def vcopy(dst, src, after=()):
        return P.op("vector", lambda e, d=dst, s=src: e.tensor_copy(d, s), after)

    def vmemset(dst, val, after=()):
        return P.op("vector", lambda e, d=dst, v=val: e.memset(d, v), after)

    def vts(dst, src, s1, s2, op0, op1=None, after=()):
        if op1 is None:
            return P.op("vector",
                        lambda e, d=dst, s=src, a=s1: e.tensor_scalar(d, s, a, None, op0),
                        after)
        return P.op("vector",
                    lambda e, d=dst, s=src, a=s1, b=s2: e.tensor_scalar(d, s, a, b, op0, op1),
                    after)

    def vtt(dst, a, b, op, after=()):
        return P.op("vector", lambda e, d=dst, x=a, y=b: e.tensor_tensor(d, x, y, op), after)

    def vstt(dst, in0, scal, in1, op0, op1, after=()):
        return P.op("vector",
                    lambda e, d=dst, x=in0, s=scal, y=in1: e.scalar_tensor_tensor(d, x, s, y, op0, op1),
                    after)

    def vrecip(dst, src, after=()):
        return P.op("vector", lambda e, d=dst, s=src: e.reciprocal(d, s), after)

    def vred(dst, src, after=()):
        return P.op("vector", lambda e, d=dst, s=src: e.reduce_sum(d, s, axis=AX), after)

    def act(dst, src, fn, bias=None, scale=None, accum=None, after=()):
        kw = {}
        if bias is not None:
            kw["bias"] = bias
        if scale is not None:
            kw["scale"] = scale
        if accum is not None:
            kw["accum_out"] = accum
        return P.op("scalar",
                    lambda e, d=dst, s=src, k=dict(kw): e.activation(d, s, fn, **k),
                    after)

    def mm(pt, lhsT, rhs, start, stop, after=()):
        return P.op("tensor",
                    lambda e, o=pt, l=lhsT, r=rhs, st=start, sp=stop:
                        e.matmul(o, l, r, start=st, stop=sp),
                    after)

    def tr(pt, src, after=(), bf=False, npart=128):
        iden = idb if bf else idt
        idap = iden[:] if npart == 128 else iden[0:npart, 0:npart]
        return P.op("tensor",
                    lambda e, o=pt, s=src, i=idap: e.transpose(o, s, i),
                    after)

    def rsqrt_chain(vcol, ucol, sdcol, rcol, acol, t_v):
        t1 = vts(ucol, vcol, EPS, None, ALU.add, after=(t_v,))
        t2 = act(sdcol, ucol, AF.Sqrt, after=(t1,))
        tk = vrecip(rcol, sdcol, (t2,))
        for _ in range(2):
            ta = vts(acol, rcol, rcol, None, ALU.mult, after=(tk,))
            tb = vtt(acol, acol, ucol, ALU.mult)
            tc = vts(acol, acol, -0.5, 1.5, ALU.mult, ALU.add)
            tk = vtt(rcol, rcol, acol, ALU.mult)
        return tk

    def ln_stats(xap, mcol, rcol, sc6, after=()):
        c_sum, c_sq = sc6[:, 0:1], sc6[:, 1:2]
        c_var, c_u, c_sd, c_a = sc6[:, 2:3], sc6[:, 3:4], sc6[:, 4:5], sc6[:, 5:6]
        t1 = vred(c_sum, xap, after)
        t2 = act(trash[:], xap, AF.Square, accum=c_sq, after=after)
        t3 = vts(mcol, c_sum, INV_D, None, ALU.mult)
        t4 = vts(c_var, c_sq, INV_D, None, ALU.mult, after=(t2,))
        t5 = vts(c_u, mcol, mcol, None, ALU.mult)
        t6 = vtt(c_var, c_var, c_u, ALU.subtract)
        return rsqrt_chain(c_var, c_u, c_sd, rcol, c_a, t6)

    def split(hi, lo, src, after=()):
        t1 = vcopy(hi, src, after)
        t2 = vtt(lo, src, hi, ALU.subtract)
        return t2

    t_id = P.dma("sync", idt[:], ident)
    t_idb = P.dma("sync", idb[:], identb)
    t_tri = P.dma("sync", triS[:], triT)
    t_invd = P.dma("sync", invdS[:], invden)
    t_cb = P.dma("sync", convbS[:], convb)
    t_mb = P.dma("sync", mixbS[:], mixb)
    t_ob = P.dma("sync", outbbc[:], _ap(outb, 0, [[0, 128], [1, D]]))
    t_bcix = P.dma("sync", bcix[:], bcidx)

    def epilogue():
        outs = []
        for j in range(16):
            outs.append(P.dma("scalar", yout[j * 128:(j + 1) * 128, :], trash[:],
                              after=(P.snap())))
        P.bar("sync", tuple(outs))
        return finish_replay(nc, P, es, sems_ctx)

    sems_ctx = []

    def finish_replay(nc, P, es, _sc):
        P.finalize()
        with es:
            with ExitStack() as ses:
                sems = {sn: ses.enter_context(nc.semaphore(sn)) for sn in SEMS}
                with nc.Block() as block:
                    @block.sync
                    def _(sync):
                        P.replay("sync", sync, sems)

                    @block.scalar
                    def _(scalar):
                        P.replay("scalar", scalar, sems)

                    @block.vector
                    def _(vector):
                        P.replay("vector", vector, sems)

                    @block.tensor
                    def _(tensor):
                        P.replay("tensor", tensor, sems)

                    @block.gpsimd
                    def _(gpsimd):
                        P.replay("gpsimd", gpsimd, sems)
        return nc

    if stage < 1:
        return epilogue()

    f0 = ExitStack()
    gtT = sb(f0, "gtT", [128, 1024])
    gtp = sb(f0, "gtp", [128, D])
    h0t = sb(f0, "h0t", [128, 2 * D])
    h0b = sb(f0, "h0b", [128, D], BF16)
    h0l = sb(f0, "h0l", [128, D], BF16)
    h0T_h = sb(f0, "h0T_h", [128, 2048], BF16)
    h0T_l = sb(f0, "h0T_l", [128, 2048], BF16)
    cwb = [sb(f0, "cwb%d" % i, [128, D]) for i in range(2)]
    cwh = [sb(f0, "cwh%d" % i, [128, D], BF16) for i in range(2)]
    cwl = [sb(f0, "cwl%d" % i, [128, D], BF16) for i in range(2)]

    t_memT = P.dma("sync",
                   _ap(gtT[:], 64, [[1024, 128], [128, 8], [1, 64]]),
                   memT)



    psb_tok = [None]
    t_h0_parts = []
    for i in range(2):
        tl = P.dma("sync", xst[i][:], x0[i * 128:(i + 1) * 128, :])
        tst = ln_stats(xst[i][:], stats[:, 40 + i:41 + i], stats[:, 42 + i:43 + i],
                       stats[:, 44:50], after=(tl,))
        ta = vts(h0t[:, i * D:(i + 1) * D], xst[i][:],
                 stats[:, 40 + i:41 + i], stats[:, 42 + i:43 + i],
                 ALU.subtract, ALU.mult)
        tsp = split(h0b[:], h0l[:], h0t[:, i * D:(i + 1) * D], (ta,))
        for half, srcb in ((0, h0b), (1, h0l)):
            tts = []
            for dc in range(8):
                tts.append(tr(psb[:, dc * 128:(dc + 1) * 128],
                              srcb[:, dc * 128:(dc + 1) * 128],
                              after=(tsp, t_idb, psb_tok[0]), bf=True))
            dstten = h0T_h if half == 0 else h0T_l
            dst = _ap(dstten[:], i * 128, [[2048, 128], [256, 8], [1, 128]])
            ev = vcopy(dst, psb[:], tuple(tts))
            psb_tok[0] = ev
            t_h0_parts.append(ev)
    t_h0 = P.bar("vector", tuple(t_h0_parts))

    def h0T_strided(hi, dc_in, k):
        ten = h0T_h if hi else h0T_l
        return _ap(ten[:], dc_in * 256 + k, [[2048, 128], [4, 64]])

    cw_tok = [None, None]
    conv_last = None
    n_conv = 4 * 8 * 3 * 8
    ic = 0
    for k in range(4):
        for dc_in in range(8):
            bi = (k * 8 + dc_in) % 2
            tl = P.dma("sync", cwb[bi][:], convwT[k, dc_in * 128:(dc_in + 1) * 128, :],
                       after=(cw_tok[bi],))
            th = vcopy(cwh[bi][:], cwb[bi][:], (tl,))
            tlo = vtt(cwl[bi][:], cwb[bi][:], cwh[bi][:], ALU.subtract)
            for (lh, rhi, dep) in ((cwh[bi], True, th), (cwh[bi], False, tlo),
                                   (cwl[bi], True, tlo)):
                rhs = h0T_strided(rhi, dc_in, k)
                for dg in range(8):
                    conv_last = mm(ps[5][:, dg * 64:(dg + 1) * 64],
                                   lh[:, dg * 128:(dg + 1) * 128], rhs,
                                   start=(ic == 0), stop=(ic == n_conv - 1),
                                   after=(dep, t_h0))
                    ic += 1
            cw_tok[bi] = P.bar("tensor", (conv_last,))
    t_conv = P.bar("tensor", (conv_last,))
    t_gt_parts = [act(gtT[:, dg * 128:dg * 128 + 64], ps[5][:, dg * 64:(dg + 1) * 64],
                      AF.Identity, bias=convbS[:, dg:dg + 1],
                      after=(t_conv, t_cb, t_memT)) for dg in range(8)]
    t_gt = P.bar("scalar", tuple(t_gt_parts))
    t_gts = split(gtT_h[:], gtT_l[:], gtT[:], (t_gt,))
    t_vg_ones = vmemset(_ap(vg[:], 64, [[16 * 65, 128], [65, 16]]), 1.0)
    t_gtp_parts = []
    for dg in range(8):
        pcol = (dg % 4) * 128
        tt = tr(ps[6][:, pcol:pcol + 128], gtT[:, dg * 128:(dg + 1) * 128],
                after=(t_gt, t_id, t_gtp_parts[-1] if t_gtp_parts else None))
        t_gtp_parts.append(vcopy(gtp[:, dg * 128:(dg + 1) * 128],
                                 ps[6][:, pcol:pcol + 128], (tt,)))
    t_gtp = P.bar("vector", tuple(t_gtp_parts))
    t_vg_parts = [vcopy(vg[:, h * 65:h * 65 + 64], gtp[:, h * 64:(h + 1) * 64],
                        (t_gtp, t_vg_ones)) for h in range(NH)]
    t_vg = P.bar("vector", tuple(t_vg_parts))
    t_ps6_0 = P.bar("tensor", (t_gtp,))
    if "gtT" in taps:
        tap_dma.append(P.dma("gpsimd", tap_d["gtT"], gtT[:], after=(t_gts, t_vg)))

    t_stats = [None] * NCH
    xst_tok = [None, None]

    def emit_ln1_stats(i):
        bi = i % 2
        tl = P.dma("sync", xst[bi][:], xslab[i * 128:(i + 1) * 128, :],
                   after=(xst_tok[bi],))
        t_stats[i] = ln_stats(xst[bi][:], stats[:, i:i + 1], stats[:, 20 + i:21 + i],
                              stats[:, 50:56], after=(tl,))
        xst_tok[bi] = t_stats[i]

    t_ag_stats = [None]

    def plan_stats_ag():
        after = tuple(t_stats[OWN0:OWN0 + NOWN])
        t1 = P.dma("gpsimd", _ap(stats_in, 0, [[1, 128], [128, 16]]),
                   stats[:, OWN0:OWN0 + 16], after=after, slow=True)
        t2 = P.dma("gpsimd", _ap(stats_in, 2048, [[1, 128], [128, 16]]),
                   stats[:, 20 + OWN0:20 + OWN0 + 16], after=after, slow=True)

        if _NO_STATS_AG:
            t_ag_stats[0] = P.bar("gpsimd", (t1, t2))
            return

        def emit_cc(e):
            return e.collective_compute(
                "AllGather", ALU.bypass, replica_groups=[list(range(NC))],
                ins=[stats_in.opt()], outs=[stats_all.opt()])
        t_ag_stats[0] = P.op("gpsimd", emit_cc, (t1, t2), sem="cc")

    if stage >= 2:
        for i in range(NCH):
            emit_ln1_stats(i)
        plan_stats_ag()

    wsp_tok = [cw_tok[0], cw_tok[1]]
    for (wsrc, whi, wlo_, nchunks) in ((mixw, mixw_hi, mixw_lo, 16),
                                       (outw, outw_hi, outw_lo, 8)):
        for kc in range(nchunks):
            bi = kc % 2
            tl = P.dma("sync", cwb[bi][:], wsrc[kc * 128:(kc + 1) * 128, :],
                       after=(wsp_tok[bi],))
            th = vcopy(cwh[bi][:], cwb[bi][:], (tl,))
            tlo = vtt(cwl[bi][:], cwb[bi][:], cwh[bi][:], ALU.subtract)
            t1 = P.dma("gpsimd", whi[kc * 128:(kc + 1) * 128, :], cwh[bi][:],
                       after=(th,))
            t2 = P.dma("gpsimd", wlo_[kc * 128:(kc + 1) * 128, :], cwl[bi][:],
                       after=(tlo,))
            wsp_tok[bi] = P.bar("vector", (t1, t2))
    t_wsplit = P.bar("vector", tuple(wsp_tok))

    t_front = P.snap()
    f0.close()
    if stage < 2:
        return epilogue()

    pa = ExitStack()
    hf = [sb(pa, "hf%d" % i, [128, 16 * 65]) for i in range(RING_HF)]
    hsb = sb(pa, "hsb", [128, D], BF16)
    hsl = sb(pa, "hsl", [128, D], BF16)
    hTh = sb(pa, "hTh", [128, RING_HT * D], BF16)
    hTl = sb(pa, "hTl", [128, RING_HT * D], BF16)
    expl = [sb(pa, "expl%d" % i, [128, 512]) for i in range(8)]
    avt = [sb(pa, "avt%d" % i, [128, 512]) for i in range(2)]
    fbuf = [sb(pa, "fbuf%d" % i, [128, 260]) for i in range(4)]
    gbuf = [sb(pa, "gbuf%d" % i, [128, 65]) for i in range(16)]
    lacc = [sb(pa, "lacc%d" % i, [128, D]) for i in range(4)]
    gacc = [sb(pa, "gacc%d" % i, [128, D]) for i in range(2)]
    bcur = sb(pa, "bcur", [128, S])
    bnxt = sb(pa, "bnxt", [128, S])
    bmr = sb(pa, "bmr", [128, 2048])

    tF = P.bar("sync", t_front)
    t_hfones = [vmemset(_ap(hf[i][:], 64, [[16 * 65, 128], [65, 16]]), 1.0,
                        (tF,)) for i in range(RING_HF)]

    def hf65(slot):
        return _ap(hf[slot][:], 0, [[16 * 65, 128], [65, 16], [1, 64]])

    t_happly = [None] * NCH
    t_hT = [None] * NCH
    hf_war = [t_hfones[i] for i in range(RING_HF)]
    hT_war = [None] * RING_HT
    xst_tok2 = [tF, tF]

    def emit_ln1_chunk(i):
        bi = i % 2
        tl = P.dma("sync", xst[bi][:], xslab[i * 128:(i + 1) * 128, :],
                   after=(xst_tok2[bi],))
        slot = i % RING_HF
        ta = vts(hf65(slot), xst[bi][:], stats[:, i:i + 1], stats[:, 20 + i:21 + i],
                 ALU.subtract, ALU.mult, after=(hf_war[slot], tl))
        t_happly[i] = ta
        xst_tok2[bi] = ta
        if "h" in taps:
            tt_ = P.dma("gpsimd", tap_d["h"][i * 128:(i + 1) * 128, :],
                        hf65(slot), after=(ta,))
            tap_dma.append(tt_)
            hf_war[slot] = P.bar("gpsimd", (tt_, hf_war[slot]))
        tslot = i % RING_HT
        tsp = split(hsb[:], hsl[:], hf65(slot), (ta,))
        t_parts = []
        for half, srcb in ((0, hsb), (1, hsl)):
            tts = [tr(psb[:, dc * 128:(dc + 1) * 128],
                      srcb[:, dc * 128:(dc + 1) * 128],
                      after=(tsp, t_idb, hT_war[tslot], psb_tok[0]), bf=True)
                   for dc in range(8)]
            dst = hTh if half == 0 else hTl
            ev = vcopy(dst[:, tslot * D:(tslot + 1) * D], psb[:], tuple(tts))
            psb_tok[0] = ev
            t_parts.append(ev)
        t_hT[i] = P.bar("vector", tuple(t_parts))
        hT_war[tslot] = Tok("ct", P.cnt["ct"])

    SH = [1, 2, 4, 8, 16, 32, 64, 128, 256, 512, 1024, 2048, 4096]
    t_res_ag = [None]

    class Bcast:
        def __init__(self):
            self.state = "apply"
            self.bb = 0
            self.j = 0
            self.it = 0
            self.tcur = None
            self.tmr = None
            self.tstore = [None, None]
            self.cur = bcur
            self.nxt = bnxt
            self.res_done = [None, None]
            self.pieces = []
            self.iter_toks = []
            self.done = False

        def quantum(self):
            if self.done:
                return False
            bb = self.bb
            if self.state == "apply":
                j = self.j
                rk = bb * 4 + j
                sl = slice(j * 2048, (j + 1) * 2048)
                t_x = P.dma("sync", self.cur[:, sl],
                            xds[:, bb * S + j * 2048: bb * S + (j + 1) * 2048],
                            after=(self.tmr, self.tstore[0], self.tstore[1]))
                t_m = P.dma("gpsimd", bmr[:],
                            _ap(stats_all, rk * 4096, [[0, 128], [1, 2048]]),
                            after=(t_ag_stats[0], self.tmr))
                t1 = vtt(self.cur[:, sl], self.cur[:, sl], bmr[:], ALU.subtract,
                         (t_m, t_x))
                t_r = P.dma("gpsimd", bmr[:],
                            _ap(stats_all, rk * 4096 + 2048, [[0, 128], [1, 2048]]),
                            after=(t1,))
                t2 = vtt(self.cur[:, sl], self.cur[:, sl], bmr[:], ALU.mult,
                         (t_r,))
                self.tmr = t2
                self.j += 1
                if self.j == 4:
                    self.j = 0
                    self.it = 0
                    self.tcur = t2
                    self.state = "iter"
                return True
            if not self.pieces:
                sh = SH[self.it]
                war = self.tstore[0]
                self.iter_toks = []
                CH = 2048
                if sh == 4096:
                    for a0 in (0, 4096):
                        s0 = (a0 + 4096) % S
                        for off in range(0, 4096, CH):
                            self.pieces.append((a0 + off, a0 + off, s0 + off,
                                                CH, war, False))
                else:
                    for (a0, ln0) in ((0, sh), (sh, S - 2 * sh), (S - sh, sh)):
                        m0 = (a0 - sh) % S
                        p0 = (a0 + sh) % S
                        for off in range(0, ln0, CH):
                            self.pieces.append((a0 + off, m0 + off, p0 + off,
                                                min(CH, ln0 - off), war, True))
            budget = 4096
            while self.pieces and budget > 0:
                a, am, ap_, ln, war, has_stt = self.pieces.pop(0)
                t1 = vtt(self.nxt[:, a:a + ln], self.cur[:, am:am + ln],
                         self.cur[:, ap_:ap_ + ln], ALU.add,
                         (self.tcur, war))
                if has_stt:
                    t1 = vstt(self.nxt[:, a:a + ln], self.nxt[:, a:a + ln],
                              0.5, self.cur[:, a:a + ln], ALU.mult, ALU.add,
                              (t1,))
                budget -= ln * (2 if has_stt else 1)
                self.iter_toks.append(t1)
            if self.pieces:
                return True
            tall = P.bar("vector", tuple(self.iter_toks))
            acc = (self.it > 0 and not _NO_ACCUM)
            sts = [P.dma("gpsimd",
                         res_in[bb][:, q * 2048:(q + 1) * 2048],
                         self.nxt[:, q * 2048:(q + 1) * 2048],
                         after=(tall, self.tstore[1]), accum=acc)
                   for q in range(4)]
            ts_ = P.bar("gpsimd", tuple(sts))
            self.tstore = [ts_, self.tstore[0]]
            self.tcur = tall
            self.cur, self.nxt = self.nxt, self.cur
            self.it += 1
            if self.it == len(SH):
                if _NO_RES_AG:
                    self.res_done[bb] = P.bar("gpsimd", (ts_,))
                else:
                    def emit_cc(e, b=bb):
                        return e.collective_compute(
                            "AllGather", ALU.bypass,
                            replica_groups=[list(range(NC))],
                            ins=[res_in[b].opt()],
                            outs=[_ap(res_ab, b * NC * 128 * S,
                                      [[1, NC * 128 * S]])])
                    self.res_done[bb] = P.op("gpsimd", emit_cc, (ts_,), sem="cc")
                self.bb += 1
                self.state = "apply"
                self.tmr = tall
                if self.bb == B:
                    self.done = True
            return True

        def finish(self):
            while self.quantum():
                pass
            t_res_ag[0] = P.bar("gpsimd", tuple(self.res_done))

    bc = Bcast()

    t_lacc = {}
    lacc_war = [None] * 4
    gacc_war = [None] * 2
    t_lspill = [None] * NOWN
    t_gspill = [None] * NOWN
    expl_war = [None] * 4
    av_war = [None, None]
    ps45_tok = [None, None]
    gexp_war = [None, None]
    gav_war = [None, None]
    gn_war = [None, None]
    lacc_init = {}
    ps4_tok = [None]
    ps6_tok = [t_ps6_0]
    fbuf_war = [None] * 4
    gbuf_war = [None] * 16

    def emit_window(w, filler=None):
        c0 = 2 * w
        qc0 = max(c0, OWN0)
        qc1 = min(c0 + 4, OWN0 + NOWN)
        nq = (qc1 - qc0) * 128

        def logits_exp(h):
            dc = h // 2
            prow = 64 * (h % 2)
            par = h % 2
            for kc in range(4):
                c = c0 + kc
                pt = ps[kc]
                kslot = c % RING_HT
                runs = []
                cc_ = qc0
                while cc_ < qc1:
                    r0_ = cc_
                    while (cc_ + 1 < qc1 and
                           (cc_ + 1) % RING_HT == (cc_ % RING_HT) + 1):
                        cc_ += 1
                    runs.append((r0_, cc_ + 1))
                    cc_ += 1
                last = None
                np_i = 0
                n_tot = len(runs) * 3
                for (ra, rb) in runs:
                    nrun = rb - ra
                    rslot = ra % RING_HT
                    for (lhi, rhi) in ((True, True), (True, False), (False, True)):
                        lten = hTh if lhi else hTl
                        rten = hTh if rhi else hTl
                        lhsT = lten[prow:prow + 64,
                                    kslot * D + dc * 128: kslot * D + dc * 128 + 128]
                        sl0 = rten[prow:prow + 64, 0:1]
                        rhs = bass.AP(tensor=sl0.tensor,
                                      offset=sl0.offset + rslot * D + dc * 128,
                                      ap=[list(sl0.ap[0]), [D, nrun], [1, 128]])
                        qcol = (ra - qc0) * 128
                        last = mm(pt[:, qcol:qcol + nrun * 128], lhsT, rhs,
                                  start=(np_i == 0), stop=(np_i == n_tot - 1),
                                  after=(t_hT[c], t_hT[rb - 1], expl_war[kc]))
                        np_i += 1
                expl_war[kc] = act(expl[4 * par + kc][:, 0:nq], pt[:, 0:nq],
                                   AF.Exp, scale=ISQ,
                                   after=(last, av_war[par]))

        def av(h, nq_h):
            par = h % 2
            av_last = None
            for kc in range(4):
                c = c0 + kc
                av_last = mm(ps[4 + par][0:65, 0:nq_h],
                             hf[c % RING_HF][:, h * 65:h * 65 + 65],
                             expl[4 * par + kc][:, 0:nq_h],
                             start=(kc == 0), stop=(kc == 3),
                             after=(expl_war[kc], t_happly[c],
                                    t_hfones[c % RING_HF], ps45_tok[par]))
            av_war[par] = av_last
            tev = act(avt[par][0:65, 0:nq_h], ps[4 + par][0:65, 0:nq_h],
                      AF.Identity, after=(av_last,))
            ps45_tok[par] = tev
            return tev

        def fold(h, tev):
            par = h % 2
            k4 = h % 4
            nj = qc1 - qc0
            tts = [tr(ps[6][:, 65 * j:65 * j + 65],
                      avt[par][0:65, j * 128:(j + 1) * 128],
                      after=(tev, t_id, ps6_tok[0]), npart=65)
                   for j in range(nj)]
            tevf = act(fbuf[k4][:, 0:65 * nj], ps[6][:, 0:65 * nj],
                       AF.Identity, after=tuple(tts) + (fbuf_war[k4],))
            ps6_tok[0] = tevf
            trc = vrecip(smm[:, 0:nj],
                         _ap(fbuf[k4][:], 64, [[260, 128], [65, nj]]),
                         (tevf,))
            jj0 = qc0 - c0
            tsc = vtt(smm[:, 4:4 + nj], smm[:, 0:nj],
                      triS[:, 4 * w + jj0:4 * w + jj0 + nj], ALU.mult,
                      (t_tri,))
            last = tsc
            for j in range(nj):
                cq = qc0 + j
                o = cq - OWN0
                slot = o % 4
                init_tok = lacc_init.get(cq)
                if init_tok is None:
                    init_tok = vmemset(lacc[slot][:], 0.0, (lacc_war[slot],))
                    lacc_init[cq] = init_tok
                last = vstt(lacc[slot][:, h * 64:(h + 1) * 64],
                            fbuf[k4][:, 65 * j:65 * j + 64],
                            smm[:, 4 + j:5 + j],
                            lacc[slot][:, h * 64:(h + 1) * 64],
                            ALU.mult, ALU.add,
                            after=(init_tok, t_lacc.get((cq, h))))
                t_lacc[(cq, h)] = last
            fbuf_war[k4] = last

        pend = {}
        for h in range(NH + 2):
            if h < NH:
                logits_exp(h)
            if 1 <= h <= NH:
                pend[h - 1] = av(h - 1, nq)
            if h >= 2:
                fold(h - 2, pend.pop(h - 2))
            if h >= 2 and filler is not None:
                filler()

    def spill_local(c):
        o = c - OWN0
        slot = o % 4
        after = tuple(t_lacc.get((c, h)) for h in range(NH))
        tl = vts(lacc[slot][:], lacc[slot][:], invdS[:, o:o + 1], None,
                 ALU.mult, after=after + (t_invd,))
        dst = tap_d["local"] if "local" in taps else local_d
        td = P.dma("gpsimd", dst[o * 128:(o + 1) * 128, :], lacc[slot][:],
                   after=(tl,))
        t_lspill[o] = td
        lacc_war[slot] = td

    def emit_glob(c):
        o = c - OWN0
        slot = o % 2
        t_init = vmemset(gacc[slot][:], 0.0, (gacc_war[slot],))
        kslot = c % RING_HT

        def glogits(h):
            par = h % 2
            dc = h // 2
            prow = 64 * (h % 2)
            last = None
            for pi, (lhi, rhi) in enumerate(((True, True), (True, False),
                                             (False, True))):
                lten = gtT_h if lhi else gtT_l
                rten = hTh if rhi else hTl
                last = mm(ps[par][:, 0:128],
                          lten[prow:prow + 64, dc * 128:(dc + 1) * 128],
                          rten[prow:prow + 64,
                               kslot * D + dc * 128: kslot * D + dc * 128 + 128],
                          start=(pi == 0), stop=(pi == 2),
                          after=(t_gts, t_hT[c], gexp_war[par]))
            gexp_war[par] = act(geL[par][:], ps[par][:, 0:128], AF.Exp,
                                scale=ISQ, after=(last, gav_war[par]))

        def gav(h):
            par = h % 2
            k4 = h
            tav = mm(ps[2 + par][:, 0:65], geL[par][:],
                     vg[:, h * 65:h * 65 + 65],
                     start=True, stop=True,
                     after=(gexp_war[par], t_vg, gn_war[par]))
            gav_war[par] = tav
            tevg = act(gbuf[k4][:], ps[2 + par][:, 0:65], AF.Identity,
                       after=(tav, gbuf_war[k4]))
            gn_war[par] = tevg
            trc = vrecip(smm[:, 2:3], gbuf[k4][:, 64:65], (tevg,))
            tn = vts(gacc[slot][:, h * 64:(h + 1) * 64], gbuf[k4][:, 0:64],
                     smm[:, 2:3], None, ALU.mult, after=(t_init,))
            gbuf_war[k4] = tn
            return tn

        tn = None
        for h in range(NH + 1):
            if h < NH:
                glogits(h)
            if h >= 1:
                tn = gav(h - 1)
        expl_war[0] = gexp_war[0]
        expl_war[1] = gexp_war[1]
        expl_war[2] = gn_war[0]
        expl_war[3] = gn_war[1]
        dst = tap_d["glob"] if "glob" in taps else glob_d
        td = P.dma("gpsimd", dst[o * 128:(o + 1) * 128, :], gacc[slot][:],
                   after=(tn,))
        t_gspill[o] = td
        gacc_war[slot] = td

    if stage >= 3:
        for i in range(NCH + 2):
            if i < NCH:
                emit_ln1_chunk(i)
            if i >= 5 and i % 2 == 1:
                w = (i - 5) // 2
                filler = (lambda: bc.quantum()) if stage >= 4 else None
                emit_window(w, filler)
                for c in (2 * w, 2 * w + 1):
                    if OWN0 <= c < OWN0 + NOWN:
                        spill_local(c)
                        emit_glob(c)
    if stage >= 4:
        bc.finish()

    if stage < 5:
        pa.close()
        return epilogue()
    t_phaseA = P.snap()
    pa.close()

    pw = ExitStack()
    w1q = sb(pw, "w1q", [128, 8 * H4], F8)
    t_w1q = [P.dma("gpsimd", _ap(w1q[:], 0, [[8 * H4, 128], [H4, 8], [1, H4]]),
                   _ap(w1T, 0, [[H4, 128], [128 * H4, 8], [1, H4]]),
                   after=t_phaseA)]

    pb = ExitStack()
    lst = sb(pb, "lst", [128, 512])
    gst = sb(pb, "gst", [128, 512])
    mixwR_h = sb(pb, "mixwR_h", [128, 16 * D], BF16)
    outwR_h = sb(pb, "outwR_h", [128, 8 * D], BF16)
    wlo = [sb(pb, "wlo%d" % i, [128, D], BF16) for i in range(2)]
    catTh = sb(pb, "catTh", [128, 16 * 512], BF16)
    catTl = sb(pb, "catTl", [128, 16 * 512], BF16)
    gateT = sb(pb, "gateT", [128, 512])
    lT = sb(pb, "lT", [128, 512])
    gT = sb(pb, "gT", [128, 512])
    bcT = sb(pb, "bcT", [128, 512])
    mixT = sb(pb, "mixT", [128, 512])
    mixTh = sb(pb, "mixTh", [128, 8 * 512], BF16)
    mixTl = sb(pb, "mixTl", [128, 8 * 512], BF16)
    aof = sb(pb, "aof", [128, D])
    x2c = sb(pb, "x2c", [128, D])
    xrs = sb(pb, "xrs", [128, D])
    cat32 = [sb(pb, "cat32_%d" % i, [128, 512]) for i in range(2)]

    local_src = tap_d["local"] if "local" in taps else local_d
    glob_src = tap_d["glob"] if "glob" in taps else glob_d

    tA = P.bar("sync", t_phaseA)
    t_mwr = P.dma("sync", mixwR_h[:],
                  _ap(mixw_hi, 0, [[D, 128], [128 * D, 16], [1, D]]), after=(tA,))
    t_owr = P.dma("sync", outwR_h[:],
                  _ap(outw_hi, 0, [[D, 128], [128 * D, 8], [1, D]]), after=(tA,))
    t_x2st = [None] * NST
    mixed_war = [None]
    gate_war = [None]
    x2c_war = [tA]
    xrs_war = [tA]
    lg_tok = [tA, tA]
    wlo_tok = [tA, tA]
    ps67_tok = [ps6_tok[0]]
    cat_war = [None]
    cat32_war = [tA, tA]

    def load_lg(buf, src_d, o0, dcc, after):
        src = _ap(src_d, o0 * 128 * D + dcc * 128,
                  [[D, 128], [128 * D, 4], [1, 128]])
        return P.dma("sync", buf[:], src, after=after)

    def emit_supertile(st_):
        ch0 = OWN0 + 4 * st_
        o0 = 4 * st_
        spills = tuple(t_lspill[o0:o0 + 4] + t_gspill[o0:o0 + 4])
        t_cat = []
        lst2 = [lst, gst]
        for kc in range(16):
            src_d = local_src if kc < 8 else glob_src
            dcc = kc % 8
            par = kc % 2
            pt = ps[4 + par]
            tld = load_lg(lst2[par], src_d, o0, dcc, spills + (lg_tok[par],))
            tt_list = [tr(pt[:, j * 128:(j + 1) * 128],
                          lst2[par][:, j * 128:(j + 1) * 128],
                          after=(tld, t_id, ps45_tok[par]))
                       for j in range(4)]
            sl = slice(kc * 512, (kc + 1) * 512)
            tse = act(cat32[par][:], pt[:, 0:512], AF.Identity,
                      after=tuple(tt_list) + (cat32_war[par],))
            ps45_tok[par] = tse
            th = vcopy(catTh[:, sl], cat32[par][:], (tse, cat_war[0]))
            tl2 = vtt(catTl[:, sl], cat32[par][:], catTh[:, sl], ALU.subtract)
            cat32_war[par] = tl2
            lg_tok[par] = P.bar("tensor", (tl2,))
            t_cat.append(tl2)
        t_catall = P.bar("vector", tuple(t_cat))
        t_split_done = []
        for grp in range(2):
            gate_last = None
            mm_all = []
            for kc in range(16):
                bi = kc % 2
                tlo = P.dma("sync", wlo[bi][:],
                            mixw_lo[kc * 128:(kc + 1) * 128, :],
                            after=(wlo_tok[bi],))
                sl = slice(kc * 512, (kc + 1) * 512)
                mmts = []
                for dg in range(4):
                    dout = grp * 4 + dg
                    wcol = kc * D + dout * 128
                    for p_, (lt_, rt_) in enumerate((
                            (mixwR_h[:, wcol:wcol + 128], catTh[:, sl]),
                            (mixwR_h[:, wcol:wcol + 128], catTl[:, sl]),
                            (wlo[bi][:, dout * 128:(dout + 1) * 128],
                             catTh[:, sl]))):
                        gate_last = mm(
                            ps[dg][:, 0:512], lt_, rt_,
                            start=(kc == 0 and p_ == 0),
                            stop=(kc == 15 and p_ == 2),
                            after=(t_mwr, tlo, t_catall) + tuple(t_split_done))
                        mmts.append(gate_last)
                mm_all.extend(mmts)
                wlo_tok[bi] = P.bar("tensor", (gate_last,))
            t_gmm = P.bar("tensor", (gate_last,))
            for dg in range(4):
                dout = grp * 4 + dg
                tg = act(gateT[:], ps[dg][:, 0:512], AF.Sigmoid,
                         bias=mixbS[:, dout:dout + 1],
                         after=(t_gmm, t_mb, gate_war[0]))
                lsl = slice(dout * 512, (dout + 1) * 512)
                gsl = slice((8 + dout) * 512, (9 + dout) * 512)
                te1 = vtt(lT[:], catTh[:, lsl], catTl[:, lsl], ALU.add, (t_catall,))
                te2 = vtt(gT[:], catTh[:, gsl], catTl[:, gsl], ALU.add)
                ixcol = st_ * 8 + dout
                tbc = P.op(
                    "gpsimd",
                    lambda e, col=ixcol: e.indirect_dma_start(
                        out=bcT[:], out_offset=None,
                        in_=_ap(res_ab, 0, [[512, B * NC * 128 * (S // 512)], [1, 512]]),
                        in_offset=bass.IndirectOffsetOnAxis(
                            ap=bcix[:, col:col + 1], axis=0)),
                    after=(t_res_ag[0], t_bcix, mixed_war[0]), sem="dg", n=16)
                tm1 = vtt(mixT[:], lT[:], gT[:], ALU.subtract)
                tm2 = vtt(mixT[:], gateT[:], mixT[:], ALU.mult, (tg,))
                gate_war[0] = tm2
                tm3 = vtt(mixT[:], mixT[:], gT[:], ALU.add)
                tm4 = vstt(mixT[:], bcT[:], 1.0 / 14.0, mixT[:], ALU.mult,
                           ALU.add, (tbc,))
                mixed_war[0] = tm4
                tsp2 = split(mixTh[:, dout * 512:(dout + 1) * 512],
                             mixTl[:, dout * 512:(dout + 1) * 512], mixT[:])
                t_split_done.append(tsp2)
                if "bc" in taps:
                    tap_dma.append(P.dma("gpsimd",
                                         tap_d["bc"][dout * 128:(dout + 1) * 128,
                                                     st_ * 512:(st_ + 1) * 512],
                                         bcT[:], after=(tm4,)))
        cat_war[0] = t_split_done[-1]
        t_mix = P.bar("vector", tuple(t_split_done))
        x2toks = []
        for j in range(4):
            cq = ch0 + j
            last = None
            for kc in range(8):
                bi = kc % 2
                tlo = P.dma("sync", wlo[bi][:],
                            outw_lo[kc * 128:(kc + 1) * 128, :],
                            after=(wlo_tok[bi],))
                for nh_ in range(2):
                    pt = ps[5 + nh_]
                    for p_, lt_ in enumerate((mixTh, mixTl, mixTh)):
                        lhsT = lt_[:, kc * 512 + j * 128: kc * 512 + (j + 1) * 128]
                        rhs = (wlo[bi][:, nh_ * 512:(nh_ + 1) * 512] if p_ == 2
                               else outwR_h[:, kc * D + nh_ * 512: kc * D + (nh_ + 1) * 512])
                        last = mm(pt[:, 0:512], lhsT, rhs,
                                  start=(kc == 0 and p_ == 0),
                                  stop=(kc == 7 and p_ == 2),
                                  after=(t_owr, tlo, t_mix, ps67_tok[0],
                                         ps45_tok[1]))
                wlo_tok[bi] = P.bar("tensor", (last,))
            t_ao = P.bar("tensor", (last,))
            ta1 = vtt(aof[:, 0:512], ps[5][:, 0:512], outbbc[:, 0:512], ALU.add,
                      (t_ao, t_ob))
            ta2 = vtt(aof[:, 512:1024], ps[6][:, 0:512], outbbc[:, 512:1024],
                      ALU.add)
            ps67_tok[0] = ta2
            tst = ln_stats(aof[:], lnst[:, 0:1], lnst[:, 1:2], lnst[:, 2:8],
                           after=(ta2,))
            trs = P.dma("sync", xrs[:], xslab[cq * 128:(cq + 1) * 128, :],
                        after=(xrs_war[0],))
            tt1 = vts(x2c[:], aof[:], lnst[:, 0:1], lnst[:, 1:2],
                      ALU.subtract, ALU.mult, after=(tst, x2c_war[0]))
            tt2 = vtt(x2c[:], x2c[:], xrs[:], ALU.add, (trs,))
            xrs_war[0] = tt2
            tso = P.dma("gpsimd", x2d[(cq - OWN0) * 128:(cq - OWN0 + 1) * 128, :],
                        x2c[:], after=(tt2,))
            x2c_war[0] = tso
            x2toks.append(tso)
            if "x2" in taps:
                tap_dma.append(P.dma("gpsimd",
                                     tap_d["x2"][(cq - OWN0) * 128:(cq - OWN0 + 1) * 128, :],
                                     x2c[:], after=(tso,)))
        t_x2st[st_] = P.bar("gpsimd", tuple(x2toks))

    for st_ in range(NST):
        emit_supertile(st_)

    t_phaseB = P.snap()
    pb.close()
    if stage < 6:
        return epilogue()

    pf = ExitStack()
    w2q = sb(pf, "w2q", [128, 32 * D], F8)
    b1S = sb(pf, "b1S", [128, 32])
    b1q8 = sb(pf, "b1q8", [128, 32], F8)
    b2bc = sb(pf, "b2bc", [128, D])
    b2q8 = sb(pf, "b2q8", [128, D], F8)
    x2s = [sb(pf, "x2s%d" % i, [128, D]) for i in range(8)]
    xf = sb(pf, "xf", [128, D])
    xt8 = [sb(pf, "xt8_%d" % i, [128, 8 * 512], F8) for i in range(2)]
    h16 = [sb(pf, "h16_%d" % i, [128, 512], F16) for i in range(2)]
    h8 = sb(pf, "h8", [128, 32 * 512], F8)
    y16 = sb(pf, "y16", [128, D], F16)
    ydf = sb(pf, "ydf", [128, D])
    msk = [sb(pf, "msk%d" % i, [128, D]) for i in range(2)]
    yo = sb(pf, "yo", [128, D])
    y32 = [sb(pf, "y32_%d" % i, [128, D]) for i in range(2)]
    xq32 = [sb(pf, "xq32_%d" % i, [128, 512]) for i in range(4)]

    pA = t_phaseB
    t_b1l = P.dma("sync", b1S[:], b1c, after=pA)
    tb1a = vcopy(b1q8[:], b1S[:], (t_b1l,))
    t_b1 = vcopy(b1S[:], b1q8[:])
    t_b2l = P.dma("sync", b2bc[:], _ap(b2, 0, [[0, 128], [1, D]]), after=pA)
    tb2a = vcopy(b2q8[:], b2bc[:], (t_b2l,))
    t_b2 = vcopy(b2bc[:], b2q8[:])

    t_w2q = P.dma("gpsimd", _ap(w2q[:], 0, [[32 * D, 128], [D, 32], [1, D]]),
                  _ap(w2T, 0, [[D, 128], [128 * D, 32], [1, D]]),
                  after=pA)
    t_wq = P.bar("gpsimd", (t_w1q[0], t_w2q))

    t_out_dma = list(tap_dma)
    msk_tok = [None, None]
    x2s_tok = [None] * 8
    h16_tok = [Tok("ct", P.cnt["ct"]), Tok("ct", P.cnt["ct"])]
    xf_war = [None]
    y_war = [None]
    ps4_tok_f = [ps45_tok[0], ps45_tok[1]]
    xq_war = [None] * 4
    y32_war = [None, None]
    h8_war = [None, None]
    t_xr = [None] * NST

    def build_x(st_):
        par = st_ % 2
        txt = []
        for j in range(4):
            cq = 4 * st_ + j
            jb = 4 * par + j
            tl = P.dma("sync", x2s[jb][:], x2d[cq * 128:(cq + 1) * 128, :],
                       after=(t_x2st[st_], x2s_tok[jb]))
            tst = ln_stats(x2s[jb][:], lnst[:, 8:9], lnst[:, 9:10], lnst[:, 10:16],
                           after=(tl,))
            ta = vts(xf[:], x2s[jb][:], lnst[:, 8:9], lnst[:, 9:10],
                     ALU.subtract, ALU.mult, after=(tst, xf_war[0]))
            parts = []
            for dg4 in range(2):
                pt = ps[4 + dg4]
                tts = [tr(pt[:, dc4 * 128:(dc4 + 1) * 128],
                          xf[:, (dg4 * 4 + dc4) * 128:(dg4 * 4 + dc4 + 1) * 128],
                          after=(ta, t_id, ps4_tok_f[dg4]))
                       for dc4 in range(4)]
                dst = _ap(xt8[par][:], (dg4 * 4) * 512 + j * 128,
                          [[8 * 512, 128], [512, 4], [1, 128]])
                qb = 2 * dg4 + j % 2
                tse = act(xq32[qb][:], pt[:, 0:512], AF.Identity,
                          after=tuple(tts) + (xq_war[qb],))
                ps4_tok_f[dg4] = tse
                tq = vcopy(dst, xq32[qb][:], (tse, h8_war[par]))
                xq_war[qb] = tq
                parts.append(tq)
            txt.append(P.bar("vector", tuple(parts)))
            xf_war[0] = txt[-1]
        t_xr[st_] = P.bar("vector", tuple(txt))

    build_x(0)
    for st_ in range(NST):
        par = st_ % 2
        t_h8_parts = []
        for hc in range(32):
            bi = hc % 2
            last = None
            for kc in range(8):
                last = mm(ps[bi][:, 0:512],
                          w1q[:, kc * H4 + hc * 128: kc * H4 + (hc + 1) * 128],
                          xt8[par][:, kc * 512:(kc + 1) * 512],
                          start=(kc == 0), stop=(kc == 7),
                          after=(t_wq, t_xr[st_], h16_tok[bi]))
            te = act(h16[bi][:], ps[bi][:, 0:512], AF.Relu,
                     bias=b1S[:, hc:hc + 1], after=(last, t_b1))
            tq = vcopy(h8[:, hc * 512:(hc + 1) * 512], h16[bi][:], (te,))
            h16_tok[bi] = tq
            t_h8_parts.append(tq)
            if "hpre" in taps and st_ == 0:
                tt_ = P.dma("gpsimd", tap_d["hpre"][hc * 128:(hc + 1) * 128, :],
                            h16[bi][:], after=(tq,))
                tap_dma.append(tt_)
                h16_tok[bi] = P.bar("gpsimd", (tt_,))
        t_h8 = P.bar("vector", tuple(t_h8_parts))
        h8_war[par] = t_h8
        if st_ + 1 < NST:
            build_x(st_ + 1)
        for j in range(4):
            cq = 4 * st_ + j
            jb = 4 * par + j
            last2 = []
            for nh_ in range(2):
                pt = ps[2 + nh_]
                last = None
                for hc in range(32):
                    last = mm(pt[:, 0:512],
                              h8[:, hc * 512 + j * 128: hc * 512 + (j + 1) * 128],
                              w2q[:, hc * D + nh_ * 512: hc * D + (nh_ + 1) * 512],
                              start=(hc == 0), stop=(hc == 31),
                              after=(t_h8, y_war[0]))
                last2.append(last)
            t_mm2 = P.bar("tensor", tuple(last2))
            bi = j % 2
            tml = P.dma("sync", msk[bi][:], masksc[cq * 128:(cq + 1) * 128, :],
                        after=(msk_tok[bi],))
            yp = j % 2
            tev2 = act(y32[yp][:, 0:512], ps[2][:, 0:512], AF.Identity,
                       after=(t_mm2, y32_war[yp]))
            tev3 = act(y32[yp][:, 512:1024], ps[3][:, 0:512], AF.Identity)
            tq1 = vtt(y16[:, 0:512], y32[yp][:, 0:512], b2bc[:, 0:512], ALU.add,
                      (tev3, t_b2))
            tq2 = vtt(y16[:, 512:1024], y32[yp][:, 512:1024], b2bc[:, 512:1024],
                      ALU.add)
            y32_war[yp] = tq2
            td = vtt(ydf[:], y16[:], msk[bi][:], ALU.mult, (tml,))
            msk_tok[bi] = td
            tst = ln_stats(ydf[:], lnst[:, 16:17], lnst[:, 17:18], lnst[:, 18:24],
                           after=(td,))
            ty1 = vts(yo[:], ydf[:], lnst[:, 16:17], lnst[:, 17:18],
                      ALU.subtract, ALU.mult, after=(tst,))
            ty2 = vtt(yo[:], yo[:], x2s[jb][:], ALU.add)
            x2s_tok[jb] = ty2
            tso = P.dma("scalar", yout[cq * 128:(cq + 1) * 128, :], yo[:],
                        after=(ty2,))
            t_out_dma.append(tso)
            y_war[0] = tev3

    P.bar("sync", tuple(t_out_dma))
    pf.close()
    pw.close()
    return finish_replay(nc, P, es, sems_ctx)


def _host_constants():
    tri = np.linspace(0.5, 1.5, 512)
    den = np.zeros(S, np.float64)
    for w in range(31):
        den[w * 256: w * 256 + 512] += tri
    invden_full = (1.0 / (den.astype(np.float32) + np.float32(1e-6))).astype(np.float32)
    import jax
    with jax.default_device(jax.devices("cpu")[0]):
        keep = np.asarray(jax.random.bernoulli(
            jax.random.key(42, impl="threefry2x32"), 0.8, (B, S, D)))
    maskval = np.where(keep, np.float32(1.25), np.float32(0.0))
    return tri.astype(np.float32), invden_full, maskval


def kernel(L0_attn_hidden_states, L0_attn_global_memory, L0_attn_conv_w,
           L0_attn_conv_b, L0_attn_mix_w, L0_attn_mix_b, L0_attn_out_w,
           L0_attn_out_b, L0_ffn_w1, L0_ffn_b1, L0_ffn_w2, L0_ffn_b2):
    import ml_dtypes
    x = np.ascontiguousarray(np.asarray(L0_attn_hidden_states, np.float32))
    gm = np.asarray(L0_attn_global_memory, np.float32)
    conv_w = np.asarray(L0_attn_conv_w, np.float32)
    conv_b = np.asarray(L0_attn_conv_b, np.float32)
    mix_w = np.ascontiguousarray(np.asarray(L0_attn_mix_w, np.float32))
    mix_b = np.asarray(L0_attn_mix_b, np.float32)
    out_w = np.ascontiguousarray(np.asarray(L0_attn_out_w, np.float32))
    out_b = np.asarray(L0_attn_out_b, np.float32)
    w1 = np.asarray(L0_ffn_w1, np.float32)
    b1 = np.asarray(L0_ffn_b1, np.float32)
    w2 = np.asarray(L0_ffn_w2, np.float32)
    b2v = np.asarray(L0_ffn_b2, np.float32)

    tri, invden_full, maskval = _host_constants()

    memT = np.ascontiguousarray(
        gm.transpose(0, 2, 1, 3).reshape(64, 1024)
          .reshape(64, 8, 128).transpose(2, 1, 0).reshape(128, 512))
    convwT = np.ascontiguousarray(conv_w.transpose(2, 1, 0))
    convbC = np.ascontiguousarray(conv_b.reshape(8, 128).T)
    mixbC = np.ascontiguousarray(mix_b.reshape(8, 128).T)
    w1T = np.ascontiguousarray(w1.T)
    b1C = np.ascontiguousarray(b1.reshape(32, 128).T)
    w2T = np.ascontiguousarray(w2.T)
    ident = np.eye(128, dtype=np.float32)
    identb = np.eye(128).astype(ml_dtypes.bfloat16)
    xdsT = np.ascontiguousarray(x.transpose(2, 0, 1).reshape(D, BS))

    in_maps = []
    pidx = np.arange(128)
    for c in range(NC):
        b = c // 4
        r0 = (c % 4) * TOK
        xs = np.zeros((SLAB, D), np.float32)
        lo, hi = r0 - HALO, r0 + TOK + HALO
        slo, shi = max(lo, 0), min(hi, S)
        xs[slo - lo: shi - lo] = x[b, slo:shi]
        triTv = np.zeros((128, 4 * NW), np.float32)
        W0 = r0 // 256 - 1
        for w in range(NW):
            if 0 <= W0 + w <= 30:
                for jj in range(4):
                    triTv[:, 4 * w + jj] = tri[jj * 128: jj * 128 + 128]
        invdC = np.ascontiguousarray(invden_full[r0:r0 + TOK].reshape(16, 128).T)
        bcixv = np.zeros((128, 32), np.int32)
        for st_ in range(NST):
            for dout in range(8):
                row = (b * NC + dout) * 128 + pidx
                bcixv[:, st_ * 8 + dout] = row * (S // 512) + (c % 4) * 4 + st_
        in_maps.append({
            "xslab": xs,
            "xds": np.ascontiguousarray(xdsT[c * 128:(c + 1) * 128]),
            "x0": np.ascontiguousarray(x[b, 0:256]),
            "memT": memT, "convwT": convwT, "convb": convbC,
            "mixw": mix_w, "mixb": mixbC, "outw": out_w, "outb": out_b,
            "w1T": w1T, "b1c": b1C, "w2T": w2T, "b2": b2v,
            "masksc": np.ascontiguousarray(maskval[b, r0:r0 + TOK]),
            "triT": triTv, "invden": invdC, "ident32": ident,
            "identb": identb, "bcidx": bcixv,
        })

    key = (tuple(sorted(_TAPS)), _STAGE)
    if key not in _PROG_CACHE:
        _PROG_CACHE[key] = build_program(key[0])
    nc = _PROG_CACHE[key]
    res = run_bass_kernel_spmd(nc, in_maps, core_ids=list(range(NC)))

    _TAP_RESULTS.clear()
    for tn in _TAPS:
        _TAP_RESULTS[tn] = [r["tap_" + tn] for r in res.results]

    out = np.empty((B, S, D), np.float32)
    for c in range(NC):
        b = c // 4
        r0 = (c % 4) * TOK
        out[b, r0:r0 + TOK] = res.results[c]["y"]
    return out
